# version 08
# baseline (speedup 1.0000x reference)
import os as _os
BC_NO_BIAS = _os.environ.get("BC_NO_BIAS") == "1"
BC_NO_TP = _os.environ.get("BC_NO_TP") == "1"
BC_NO_DMATP = _os.environ.get("BC_NO_DMATP") == "1"
BC_LEVEL = int(_os.environ.get("BC_LEVEL", "5"))
BC_L3SUB = _os.environ.get("BC_L3SUB", "")
import sys
if "/opt/trn_rl_repo" not in sys.path:
    sys.path.insert(0, "/opt/trn_rl_repo")
from contextlib import ExitStack

import numpy as np
import ml_dtypes

import concourse.bass as bass
import concourse.mybir as mybir
from concourse.bass_utils import run_bass_kernel_spmd

F32 = mybir.dt.float32
BF16 = mybir.dt.bfloat16
I32 = mybir.dt.int32
AF = mybir.ActivationFunctionType
ALU = mybir.AluOpType
AX = mybir.AxisListType

NSTEP = 30
EPS = 1e-5
OPS_D_TILE = 54 + 1 + 16 * 2 + 1 + 256 * 2


def make_cfg(ncores=8, R=2048, H=4096, II=11008, O=4096):
    c = {}
    c["ncores"], c["R"], c["H"], c["II"], c["O"] = ncores, R, H, II, O
    c["HALF"] = R // 2
    c["SUBH"] = c["HALF"] // 128
    assert c["SUBH"] % 2 == 0 or c["SUBH"] == 1
    c["GSZ"] = max(1, c["SUBH"] // 2)
    c["NSG"] = c["SUBH"] // c["GSZ"]
    c["NKH"] = H // 128
    ch = []
    i0 = 0
    while i0 < II:
        ch.append((i0, min(512, II - i0)))
        i0 += 512
    c["CHUNKS"] = ch
    c["NCH"] = len(ch)
    c["W32"] = H // 32
    c["OSH"] = O // ncores
    c["NOT"] = c["OSH"] // 128
    c["KTOP"] = II // 2
    c["KGRP"] = 4 if c["NKH"] % 4 == 0 else c["NKH"]
    c["NGRP"] = c["NKH"] // c["KGRP"]
    c["RB"] = min(512, R)
    c["NBLK"] = R // c["RB"]
    c["NSB"] = c["RB"] // 128
    c["NH_B"] = H // 512
    c["PAIR"] = 2 if (c["NH_B"] % 2 == 0 and c["NSB"] * 2 <= 8) else 1
    c["NGB"] = c["NH_B"] // c["PAIR"]
    c["NKI"] = II // 128
    c["NO_C"] = O // 512
    c["NS_A"] = 3
    c["NS_M"] = 3
    c["NS_D"] = 3
    c["NS_L"] = 3
    c["NS_H"] = 3
    c["NSUB"] = R // 128
    return c


class Ctr:
    def __init__(self, handle, name):
        self.h = handle
        self.n = 0
        self.name = name

    def inc(self, inst, k=1):
        inst.then_inc(self.h, k)
        self.n += k
        return self.n


def build(cfg, phase="AS"):
    R, H, II, O = cfg["R"], cfg["H"], cfg["II"], cfg["O"]
    HALF, SUBH, GSZ, NSG = cfg["HALF"], cfg["SUBH"], cfg["GSZ"], cfg["NSG"]
    NKH, CHUNKS, NCH = cfg["NKH"], cfg["CHUNKS"], cfg["NCH"]
    W32, OSH, NOT, KTOP = cfg["W32"], cfg["OSH"], cfg["NOT"], cfg["KTOP"]
    KGRP, NGRP = cfg["KGRP"], cfg["NGRP"]
    RB, NBLK, NSB, PAIR, NGB, NKI, NO_C = (cfg["RB"], cfg["NBLK"], cfg["NSB"],
                                           cfg["PAIR"], cfg["NGB"], cfg["NKI"],
                                           cfg["NO_C"])
    NS_A, NS_M, NS_D, NS_L, NS_H = (cfg["NS_A"], cfg["NS_M"], cfg["NS_D"],
                                    cfg["NS_L"], cfg["NS_H"])
    NSUB = cfg["NSUB"]
    ncores = cfg["ncores"]

    nc = bass.Bass("TRN2", target_bir_lowering=False)

    if phase == "AS":
        x_in = None
        xT_in = nc.dram_tensor("xT", [H, R], F32, kind="ExternalInput").ap()
        wgT_in = nc.dram_tensor("wgT", [H, II], F32, kind="ExternalInput").ap()
        wuT_in = nc.dram_tensor("wuT", [H, II], F32, kind="ExternalInput").ap()
        wdT_in = nc.dram_tensor("wdT", [II, H], F32, kind="ExternalInput").ap()
        qw_in = nc.dram_tensor("qw", [8, OSH, W32], I32, kind="ExternalInput").ap()
        lut4_in = nc.dram_tensor("lut4", [OSH, 16], F32, kind="ExternalInput").ap()
        lut8_in = nc.dram_tensor("lut8", [OSH, 256], F32, kind="ExternalInput").ap()
        out_t = None
        wgT_bf = nc.dram_tensor("wgT_bf", [H, II], BF16, kind="Internal").ap()
        wuT_bf = nc.dram_tensor("wuT_bf", [H, II], BF16, kind="Internal").ap()
        wdT_bf = nc.dram_tensor("wdT_bf", [II, H], BF16, kind="ExternalOutput").ap()
        spP = nc.dram_tensor("spP", [R, II], BF16, kind="Internal").ap()
        spA = nc.dram_tensor("spA", [R, II], F32, kind="Internal").ap()
        spM = nc.dram_tensor("spM", [R, II], BF16, kind="ExternalOutput").ap()
        wl_sh = nc.dram_tensor("wl_sh", [OSH, H], BF16, kind="ExternalOutput").ap()
        wh_sh = nc.dram_tensor("wh_sh", [OSH, H], BF16, kind="ExternalOutput").ap()
        wl_nat = wh_nat = None
    else:
        x_in = nc.dram_tensor("x", [R, H], F32, kind="ExternalInput").ap()
        xT_in = wgT_in = wuT_in = wdT_in = qw_in = lut4_in = lut8_in = None
        spM = nc.dram_tensor("spM", [R, II], BF16, kind="ExternalInput").ap()
        wdT_bf = nc.dram_tensor("wdT_bf", [II, H], BF16, kind="ExternalInput").ap()
        wl_nat = nc.dram_tensor("wl_nat", [O, H], BF16, kind="ExternalInput").ap()
        wh_nat = nc.dram_tensor("wh_nat", [O, H], BF16, kind="ExternalInput").ap()
        out_t = nc.dram_tensor("out", [R, H], F32, kind="ExternalOutput").ap()
        spP = spA = wl_sh = wh_sh = None
    bias_in = nc.dram_tensor("bias", [1, O], F32, kind="ExternalInput").ap()
    identf_in = nc.dram_tensor("identf", [128, 128], F32, kind="ExternalInput").ap()

    with ExitStack() as top:
        def mksem(name):
            return Ctr(top.enter_context(nc.semaphore(name)), name)

        s_identf = top.enter_context(nc.sbuf_tensor("s_identf", [128, 128], F32))
        s_ones = top.enter_context(nc.sbuf_tensor("s_ones", [1, 128], BF16))
        s_biasb = top.enter_context(nc.sbuf_tensor("s_biasb", [1, O], BF16))
        pbank = top.enter_context(nc.psum_tensor("pbank", [128, 4096], F32))

        semG0 = mksem("semG0")
        semCst = mksem("semCst")
        semWc = [mksem(f"semWc{i}") for i in range(4)]
        semWD = mksem("semWD")
        semX = mksem("semX")
        semPanS = [mksem(f"semPanS{i}") for i in range(NS_A)]
        semKgrp = mksem("semKgrp")
        semSilu = mksem("semSilu")
        semUC = mksem("semUC")
        semAbs = mksem("semAbs")
        semP = mksem("semP")
        semSpAp = [mksem(f"semSpAp{i}") for i in range(NSG)]
        semSpPp = [mksem(f"semSpPp{i}") for i in range(NSG)]
        semDL = mksem("semDL")
        semIO = mksem("semIO")
        semDV = mksem("semDV")
        semDWl = mksem("semDWl")
        semDWh = mksem("semDWh")
        semAG = mksem("semAG")
        semSL = mksem("semSL")
        semSV = mksem("semSV")
        semSP2 = mksem("semSP2")
        semSM = mksem("semSM")
        semBPm = [mksem(f"semBPm{i}") for i in range(NS_M)]
        semBPd = [mksem(f"semBPd{i}") for i in range(NS_D)]
        semXB2 = [mksem(f"semXB2{i}") for i in range(2)]
        semBk = mksem("semBk")
        semBE = mksem("semBE")
        semST = mksem("semST")
        semRS = mksem("semRS")
        semNW = mksem("semNW")
        semLNx = mksem("semLNx")
        semSQ = mksem("semSQ")
        semS2 = mksem("semS2")
        semMsk = mksem("semMsk")
        semTr = mksem("semTr")
        semTc = mksem("semTc")
        semCPl = [mksem(f"semCPl{i}") for i in range(NS_L)]
        semCPh = [mksem(f"semCPh{i}") for i in range(NS_H)]
        semCk = mksem("semCk")
        semCE = mksem("semCE")
        semLNo = mksem("semLNo")
        semOut = mksem("semOut")

        OPS_SUB = 2 + NSTEP * 12 + 1

        if phase == "AS":
            with ExitStack() as ph:
                def sb(name, shape, dt, **kw):
                    return ph.enter_context(nc.sbuf_tensor(name, shape, dt, **kw))
                a_xT = sb("a_xT", [128, NKH * HALF], BF16)
                a_ring = sb("a_ring", [128, NS_A * KGRP * 512], BF16)
                s_sb = sb("s_sb", [128, NSG * GSZ * 512], F32)
                u_sb = sb("u_sb", [128, NSG * GSZ * 512], BF16)
                a_st = sb("a_st", [128, NSG * GSZ * 512], F32)
                P_st = sb("P_st", [128, NSG * GSZ * 512], BF16)
                d_qw = sb("d_qw", [128, 8 * W32], I32)
                d_shp = sb("d_shp", [128, 32], I32)
                d_bt = sb("d_bt", [128, H // 2], I32)
                d_acc = sb("d_acc", [128, H // 2], I32)
                d_idx8 = sb("d_idx8", [128, H], BF16)
                d_idx4 = sb("d_idx4", [128, H], BF16)
                d_l4 = sb("d_l4", [128, 16], F32)
                d_l8 = sb("d_l8", [128, 256], F32)
                d_tmp = sb("d_tmp", [128, H], BF16)
                d_wl = sb("d_wl", [128, H], BF16)
                d_wh = sb("d_wh", [128, H], BF16)

                blk = ph.enter_context(nc.Block("phA"))

                NCAST = 2 * NCH
                CPC = 3
                def castv(c, mi):
                    return 16 * (2 * c + mi + 1)

                def cmi(h, c, mi):
                    return (h * NCH + c) * 2 + mi

                def sgi(h, c, sg):
                    return (h * NCH + c) * 2 + sg

                @blk.gpsimd
                def _(gp):
                    semCst.inc(gp.dma_start(out=s_biasb[:], in_=bias_in[:]), 16)
                    gp.memset(s_ones[:], 1.0)
                    semIO.inc(gp.iota(d_shp[:], pattern=[[1, 32]], base=0,
                                      channel_multiplier=0,
                                      allow_small_or_imprecise_dtypes=True))
                    NXG = max(1, NKH // 4)
                    for g in range(NXG):
                        k0 = g * (NKH // NXG)
                        kn = NKH // NXG
                        semX.inc(gp.dma_start(
                            out=a_xT[:, k0 * HALF:(k0 + kn) * HALF]
                                .rearrange("p (a n) -> p a n", a=kn),
                            in_=xT_in[k0 * 128:(k0 + kn) * 128, 0:HALF]
                                .rearrange("(a p) n -> p a n", p=128)), 16)
                    for c in range(min(CPC, NCH)):
                        i0, ic = CHUNKS[c]
                        semWc[c % 4].inc(gp.dma_start(out=wgT_bf[:, i0:i0 + ic],
                                                      in_=wgT_in[:, i0:i0 + ic]), 16)
                        semWc[c % 4].inc(gp.dma_start(out=wuT_bf[:, i0:i0 + ic],
                                                      in_=wuT_in[:, i0:i0 + ic]), 16)
                    def d_loads(ot):
                        o0 = ot * 128
                        semDL.inc(gp.dma_start(
                            out=d_qw[:].rearrange("p (a b) -> p a b", a=8, b=W32),
                            in_=qw_in[:, o0:o0 + 128, :].transpose([1, 0, 2])), 16)
                        semDL.inc(gp.dma_start(out=d_l4[:], in_=lut4_in[o0:o0 + 128, :]), 16)
                        semDL.inc(gp.dma_start(out=d_l8[:], in_=lut8_in[o0:o0 + 128, :]), 16)
                    d_loads(0)
                    nwd = 0
                    d_load_pts = {min(2 * NCH - 1, ot * (2 * NCH) // NOT): ot
                                  for ot in range(1, NOT)}
                    wd_chunk = max(1, II // 8 // 128 * 128)
                    wd_rows = []
                    r0 = 0
                    while r0 < II:
                        wd_rows.append((r0, min(wd_chunk, II - r0)))
                        r0 += wd_chunk
                    for h in range(2):
                        for c in range(NCH):
                            ch = h * NCH + c
                            if h == 0 and c + CPC < NCH:
                                i0, ic = CHUNKS[c + CPC]
                                cs = (c + CPC) % 4
                                semWc[cs].inc(gp.dma_start(out=wgT_bf[:, i0:i0 + ic],
                                                           in_=wgT_in[:, i0:i0 + ic]), 16)
                                semWc[cs].inc(gp.dma_start(out=wuT_bf[:, i0:i0 + ic],
                                                           in_=wuT_in[:, i0:i0 + ic]), 16)
                            if h == 0 and nwd < len(wd_rows) and c >= 2:
                                rr, rn = wd_rows[nwd]
                                semWD.inc(gp.dma_start(out=wdT_bf[rr:rr + rn, :],
                                                       in_=wdT_in[rr:rr + rn, :]), 16)
                                nwd += 1
                            if ch in d_load_pts:
                                ot = d_load_pts[ch]
                                gp.wait_ge(semDV.h, ot * OPS_D_TILE)
                                d_loads(ot)
                            for sg in range(NSG):
                                g_idx = sgi(h, c, sg)
                                gp.wait_ge(semUC.h, g_idx + 1)
                                gp.wait_ge(semSilu.h, g_idx + 1)
                                if ch >= 1:
                                    gp.wait_ge(semSpPp[sg].h, 16 * ch)
                                semP.inc(gp.tensor_tensor(
                                    P_st[:, sg * GSZ * 512:(sg + 1) * GSZ * 512],
                                    s_sb[:, sg * GSZ * 512:(sg + 1) * GSZ * 512],
                                    u_sb[:, sg * GSZ * 512:(sg + 1) * GSZ * 512],
                                    op=ALU.mult))
                        if h == 0:
                            while nwd < len(wd_rows):
                                rr, rn = wd_rows[nwd]
                                semWD.inc(gp.dma_start(out=wdT_bf[rr:rr + rn, :],
                                                       in_=wdT_in[rr:rr + rn, :]), 16)
                                nwd += 1
                            gp.wait_ge(semKgrp.h, 2 * NCH * NGRP)
                            for g in range(NXG):
                                k0 = g * (NKH // NXG)
                                kn = NKH // NXG
                                semX.inc(gp.dma_start(
                                    out=a_xT[:, k0 * HALF:(k0 + kn) * HALF]
                                        .rearrange("p (a n) -> p a n", a=kn),
                                    in_=xT_in[k0 * 128:(k0 + kn) * 128, HALF:R]
                                        .rearrange("(a p) n -> p a n", p=128)), 16)
                    gp.wait_ge(semDV.h, NOT * OPS_D_TILE)
                    gp.wait_ge(semDWl.h, 16 * NOT)
                    gp.wait_ge(semDWh.h, 16 * NOT)
                    gp.wait_ge(semWD.h, semWD.n)
                    assert semP.n == 2 * NCH * NSG

                @blk.sync
                def _(sy):
                    semG0.inc(sy.dma_start(out=s_identf[:], in_=identf_in[:]), 16)
                    for h in range(2):
                        for c in range(NCH):
                            i0, ic = CHUNKS[c]
                            for mi in range(2):
                                nat = wgT_bf if mi == 0 else wuT_bf
                                sy.wait_ge(semWc[c % 4].h, 32 * (c // 4 + 1))
                                for kg in range(NGRP):
                                    gi = cmi(h, c, mi) * NGRP + kg
                                    if gi >= NS_A:
                                        sy.wait_ge(semKgrp.h, gi - NS_A + 1)
                                    slot = gi % NS_A
                                    k0 = kg * KGRP
                                    semPanS[slot].inc(sy.dma_start(
                                        out=a_ring[:, slot * KGRP * 512:
                                                   slot * KGRP * 512 + KGRP * ic]
                                            .rearrange("p (a n) -> p a n", a=KGRP),
                                        in_=nat[k0 * 128:(k0 + KGRP) * 128, i0:i0 + ic]
                                            .rearrange("(a p) n -> p a n", p=128)), 16)

                @blk.tensor
                def _(te):
                    for h in range(2):
                        te.wait_ge(semX.h, 16 * max(1, NKH // 4) * (h + 1))
                        for c in range(NCH):
                            i0, ic = CHUNKS[c]
                            ch = h * NCH + c
                            for mi in range(2):
                                gidx = cmi(h, c, mi)
                                for kg in range(NGRP):
                                    gi = gidx * NGRP + kg
                                    slot = gi % NS_A
                                    te.wait_ge(semPanS[slot].h, 16 * (gi // NS_A + 1))
                                    for kk in range(KGRP):
                                        k = kg * KGRP + kk
                                        for sg in range(NSG):
                                            if k == 0:
                                                if mi == 1:
                                                    te.wait_ge(semSilu.h, sgi(h, c, sg) + 1)
                                                elif ch > 0:
                                                    te.wait_ge(semUC.h, sgi(h, c - 1, sg) + 1
                                                               if c > 0 else
                                                               sgi(h - 1, NCH - 1, sg) + 1)
                                            for s in range(GSZ):
                                                j = sg * GSZ + s
                                                inst = te.matmul(
                                                    pbank[:, j * 512:j * 512 + ic],
                                                    a_xT[:, k * HALF + j * 128:
                                                         k * HALF + j * 128 + 128],
                                                    a_ring[:, slot * KGRP * 512 + kk * ic:
                                                           slot * KGRP * 512 + (kk + 1) * ic],
                                                    start=(k == 0), stop=(k == NKH - 1))
                                    semKgrp.inc(inst)
                    assert semKgrp.n == 4 * NCH * NGRP

                @blk.scalar
                def _(sc):
                    d_store_pts = {min(2 * NCH - 1, (ot + 1) * (2 * NCH) // NOT): ot
                                   for ot in range(NOT)}
                    for h in range(2):
                        for c in range(NCH):
                            i0, ic = CHUNKS[c]
                            ch = h * NCH + c
                            if ch in d_store_pts:
                                ot = d_store_pts[ch]
                                o0 = ot * 128
                                sc.wait_ge(semDV.h, ot * OPS_D_TILE + 54 + 1 + 32)
                                semDWl.inc(sc.dma_start(out=wl_sh[o0:o0 + 128, :],
                                                        in_=d_wl[:]), 16)
                                sc.wait_ge(semDV.h, (ot + 1) * OPS_D_TILE)
                                semDWh.inc(sc.dma_start(out=wh_sh[o0:o0 + 128, :],
                                                        in_=d_wh[:]), 16)
                            for sg in range(NSG):
                                g_idx = sgi(h, c, sg)
                                sc.wait_ge(semKgrp.h, (cmi(h, c, 0) + 1) * NGRP)
                                if g_idx >= NSG:
                                    sc.wait_ge(semP.h, g_idx - NSG + 1)
                                for s in range(GSZ):
                                    j = sg * GSZ + s
                                    inst = sc.activation(
                                        s_sb[:, j * 512:j * 512 + ic],
                                        pbank[:, j * 512:j * 512 + ic], AF.Silu)
                                semSilu.inc(inst)
                                sc.wait_ge(semSilu.h, semSilu.n)
                                if ch >= 1:
                                    sc.wait_ge(semSpAp[sg].h, 16 * ch)
                                for s in range(GSZ):
                                    j = sg * GSZ + s
                                    inst = sc.activation(
                                        a_st[:, j * 512:j * 512 + ic],
                                        s_sb[:, j * 512:j * 512 + ic], AF.Abs)
                                semAbs.inc(inst)
                                sc.wait_ge(semAbs.h, semAbs.n)
                                r0 = h * HALF + sg * GSZ * 128
                                semSpAp[sg].inc(sc.dma_start(
                                    out=spA[r0:r0 + GSZ * 128, i0:i0 + ic]
                                        .rearrange("(s p) n -> p s n", p=128),
                                    in_=a_st[:, sg * GSZ * 512:(sg + 1) * GSZ * 512]
                                        .rearrange("p (s n) -> p s n", n=512)[:, :, 0:ic]), 16)
                            for sg in range(NSG):
                                g_idx = sgi(h, c, sg)
                                sc.wait_ge(semKgrp.h, (cmi(h, c, 1) + 1) * NGRP)
                                if g_idx >= NSG:
                                    sc.wait_ge(semP.h, g_idx - NSG + 1)
                                for s in range(GSZ):
                                    j = sg * GSZ + s
                                    inst = sc.copy(
                                        u_sb[:, j * 512:j * 512 + ic],
                                        pbank[:, j * 512:j * 512 + ic])
                                semUC.inc(inst)
                                sc.wait_ge(semUC.h, semUC.n)
                            for sg in range(NSG):
                                g_idx = sgi(h, c, sg)
                                sc.wait_ge(semP.h, g_idx + 1)
                                r0 = h * HALF + sg * GSZ * 128
                                semSpPp[sg].inc(sc.dma_start(
                                    out=spP[r0:r0 + GSZ * 128, i0:i0 + ic]
                                        .rearrange("(s p) n -> p s n", p=128),
                                    in_=P_st[:, sg * GSZ * 512:(sg + 1) * GSZ * 512]
                                        .rearrange("p (s n) -> p s n", n=512)[:, :, 0:ic]), 16)
                    for ot in range(NOT):
                        if semDWl.n < 16 * (ot + 1):
                            o0 = ot * 128
                            sc.wait_ge(semDV.h, ot * OPS_D_TILE + 54 + 1 + 32)
                            semDWl.inc(sc.dma_start(out=wl_sh[o0:o0 + 128, :],
                                                    in_=d_wl[:]), 16)
                            sc.wait_ge(semDV.h, (ot + 1) * OPS_D_TILE)
                            semDWh.inc(sc.dma_start(out=wh_sh[o0:o0 + 128, :],
                                                    in_=d_wh[:]), 16)
                    assert semDWl.n == 16 * NOT and semDWh.n == 16 * NOT
                    for sg in range(NSG):
                        assert semSpAp[sg].n == 16 * 2 * NCH
                        assert semSpPp[sg].n == 16 * 2 * NCH

                @blk.vector
                def _(vec):
                    def inc(inst):
                        semDV.inc(inst)
                        vec.wait_ge(semDV.h, semDV.n)
                    vec.wait_ge(semIO.h, 1)
                    W2 = W32 // 2
                    H2 = H // 2
                    for ot in range(NOT):
                        vec.wait_ge(semDL.h, 48 * (ot + 1))
                        shp_b = d_shp[:].unsqueeze(1).broadcast_to([128, W2, 32])
                        for hh in range(2):
                            for p in range(8):
                                qb = d_qw[:, p * W32 + hh * W2:p * W32 + (hh + 1) * W2] \
                                    .unsqueeze(2).broadcast_to([128, W2, 32])
                                inc(vec.tensor_tensor(
                                    d_bt[:].rearrange("p (w b) -> p w b", b=32), qb, shp_b,
                                    op=ALU.logical_shift_right))
                                inc(vec.tensor_scalar(d_bt[:], d_bt[:], 1, p,
                                                      op0=ALU.bitwise_and,
                                                      op1=ALU.logical_shift_left))
                                if p == 0:
                                    inc(vec.tensor_copy(d_acc[:], d_bt[:]))
                                else:
                                    inc(vec.tensor_tensor(d_acc[:], d_acc[:], d_bt[:],
                                                          op=ALU.bitwise_or))
                            inc(vec.tensor_copy(d_idx8[:, hh * H2:(hh + 1) * H2], d_acc[:]))
                            inc(vec.tensor_scalar(d_bt[:], d_acc[:], 15, None,
                                                  op0=ALU.bitwise_and))
                            inc(vec.tensor_copy(d_idx4[:, hh * H2:(hh + 1) * H2], d_bt[:]))
                        if ot > 0:
                            vec.wait_ge(semDWl.h, 16 * ot)
                        inc(vec.memset(d_wl[:], 0.0))
                        for cc in range(16):
                            inc(vec.tensor_scalar(d_tmp[:], d_idx4[:], float(cc),
                                                  d_l4[:, cc:cc + 1],
                                                  op0=ALU.is_equal, op1=ALU.mult))
                            inc(vec.tensor_tensor(d_wl[:], d_wl[:], d_tmp[:], op=ALU.add))
                        if ot > 0:
                            vec.wait_ge(semDWh.h, 16 * ot)
                        inc(vec.memset(d_wh[:], 0.0))
                        for cc in range(256):
                            inc(vec.tensor_scalar(d_tmp[:], d_idx8[:], float(cc),
                                                  d_l8[:, cc:cc + 1],
                                                  op0=ALU.is_equal, op1=ALU.mult))
                            inc(vec.tensor_tensor(d_wh[:], d_wh[:], d_tmp[:], op=ALU.add))
                        assert semDV.n == (ot + 1) * OPS_D_TILE


            A_SPA_FULL = [c.n for c in semSpAp]
            A_SPP_FULL = [c.n for c in semSpPp]

            with ExitStack() as ph:
                def sb(name, shape, dt, **kw):
                    return ph.enter_context(nc.sbuf_tensor(name, shape, dt, **kw))
                s_a = sb("s_a", [128, II], F32)
                s_P = sb("s_P", [128, II], BF16)
                s_m = sb("s_m", [128, II], BF16)
                s_tr1 = sb("s_tr1", [128, (II + 1) // 2], F32)
                s_tr2 = sb("s_tr2", [128, II - (II + 1) // 2], F32)
                s_lo = sb("s_lo", [128, 1], F32)
                s_hi = sb("s_hi", [128, 1], F32)
                s_mid = sb("s_mid", [128, 1], F32)
                s_c1 = sb("s_c1", [128, 1], F32)
                s_c2 = sb("s_c2", [128, 1], F32)
                s_sel = sb("s_sel", [128, 1], F32)
                s_inv = sb("s_inv", [128, 1], F32)
                s_d = sb("s_d", [128, 1], F32)
                s_nm = sb("s_nm", [128, 1], F32)
                s_cs = sb("s_cs", [128, 1], F32)

                NH1 = (II + 1) // 2

                blk = ph.enter_context(nc.Block("phS"))

                @blk.sync
                def _(sy):
                    for sg in range(NSG):
                        sy.wait_ge(semSpAp[sg].h, A_SPA_FULL[sg])
                        sy.wait_ge(semSpPp[sg].h, A_SPP_FULL[sg])
                    for t in range(NSUB):
                        r0 = t * 128
                        if t > 0:
                            sy.wait_ge(semSV.h, t * OPS_SUB)
                        semSL.inc(sy.dma_start(out=s_a[:], in_=spA[r0:r0 + 128, :]), 16)
                        semSL.inc(sy.dma_start(out=s_P[:], in_=spP[r0:r0 + 128, :]), 16)
                        sy.wait_ge(semSV.h, (t + 1) * OPS_SUB)
                        semSM.inc(sy.dma_start(out=spM[r0:r0 + 128, :], in_=s_m[:]), 16)
                    sy.wait_ge(semSM.h, 16 * NSUB)
                    sy.wait_ge(semDWl.h, 16 * NOT)
                    sy.wait_ge(semDWh.h, 16 * NOT)

                @blk.vector
                def _(vec):
                    def inc(inst):
                        semSV.inc(inst)
                        vec.wait_ge(semSV.h, semSV.n)
                    for t in range(NSUB):
                        base = t * OPS_SUB
                        vec.wait_ge(semSL.h, 32 * (t + 1))
                        inc(vec.memset(s_lo[:], 0.0))
                        inc(vec.tensor_reduce(s_hi[:], s_a[:], axis=AX.X, op=ALU.max))
                        for i in range(NSTEP):
                            inc(vec.tensor_tensor(s_mid[:], s_lo[:], s_hi[:], op=ALU.add))
                            inc(vec.tensor_scalar(s_mid[:], s_mid[:], 0.5, None,
                                                  op0=ALU.mult))
                            inc(vec.tensor_scalar(s_nm[:], s_mid[:], -1.0, None,
                                                  op0=ALU.mult))
                            semSV.inc(vec.tensor_scalar(
                                s_tr1[:], s_a[:, 0:NH1], s_mid[:], None,
                                op0=ALU.is_le, op1=ALU.add, accum_out=s_c1[:]))
                            vec.wait_ge(semSP2.h, t * NSTEP + i + 1)
                            vec.wait_ge(semSV.h, semSV.n)
                            inc(vec.tensor_scalar(s_cs[:], s_c2[:], -0.5,
                                                  float(II - NH1) * 0.5,
                                                  op0=ALU.mult, op1=ALU.add))
                            inc(vec.tensor_tensor(s_c1[:], s_c1[:], s_cs[:], op=ALU.add))
                            inc(vec.tensor_scalar(s_sel[:], s_c1[:], float(KTOP) - 0.9,
                                                  None, op0=ALU.is_ge))
                            inc(vec.tensor_scalar(s_inv[:], s_sel[:], -1.0, 1.0,
                                                  op0=ALU.mult, op1=ALU.add))
                            inc(vec.tensor_tensor(s_d[:], s_mid[:], s_hi[:],
                                                  op=ALU.subtract))
                            inc(vec.scalar_tensor_tensor(s_hi[:], s_d[:], s_sel[:],
                                                         s_hi[:], op0=ALU.mult,
                                                         op1=ALU.add))
                            inc(vec.tensor_tensor(s_d[:], s_mid[:], s_lo[:],
                                                  op=ALU.subtract))
                            inc(vec.scalar_tensor_tensor(s_lo[:], s_d[:], s_inv[:],
                                                         s_lo[:], op0=ALU.mult,
                                                         op1=ALU.add))
                        if t > 0:
                            vec.wait_ge(semSM.h, 16 * t)
                        inc(vec.scalar_tensor_tensor(s_m[:], s_a[:], s_hi[:], s_P[:],
                                                     op0=ALU.is_gt, op1=ALU.mult))
                        assert semSV.n == (t + 1) * OPS_SUB

                @blk.scalar
                def _(sc):
                    for t in range(NSUB):
                        base = t * OPS_SUB
                        sc.wait_ge(semSL.h, 32 * (t + 1))
                        for i in range(NSTEP):
                            sc.wait_ge(semSV.h, base + 2 + 12 * i + 3)
                            semSP2.inc(sc.activation(
                                s_tr2[:], s_a[:, NH1:II], AF.Sign, bias=s_nm[:],
                                accum_out=s_c2[:]))

            S_SM_FULL = semSM.n
            S_SV_FULL = semSV.n

        else:
            with ExitStack() as ph:
                def sb(name, shape, dt, **kw):
                    return ph.enter_context(nc.sbuf_tensor(name, shape, dt, **kw))
                b_z = sb("b_z", [128, NSB * H], F32)
                b_x1 = sb("b_x1", [128, NSB * H], F32)
                b_x1T = sb("b_x1T", [128, NSB * H], BF16)
                b_xp = sb("b_xp", [128, NSB * PAIR * 512], F32)
                b_rM = sb("b_rM", [128, NS_M * 512], BF16, is_transpose_dest=True)
                b_rD = sb("b_rD", [128, NS_D * PAIR * 512], BF16)
                b_rL = sb("b_rL", [128, NS_L * 512], BF16, is_transpose_dest=True)
                b_rH = sb("b_rH", [128, NS_H * 512], BF16, is_transpose_dest=True)
                b_t1 = sb("b_t1", [128, 512], F32)
                c_S1 = sb("c_S1", [128, 1], F32)
                c_S2 = sb("c_S2", [128, 1], F32)
                c_M = sb("c_M", [128, 1], F32)
                c_T = sb("c_T", [128, 1], F32)
                c_V = sb("c_V", [128, 1], F32)
                c_VE = sb("c_VE", [128, 1], F32)
                c_L = sb("c_L", [128, 1], F32)
                c_R0 = sb("c_R0", [128, 1], F32)
                c_R1 = sb("c_R1", [128, 1], F32)
                c_SSQ = sb("c_SSQ", [128, NSB], F32)
                c_mask = sb("c_mask", [128, NSB], F32)
                c_inv = sb("c_inv", [128, NSB], F32)

                blk = ph.enter_context(nc.Block("phBC"))

                WD_FULL = semWD.n

                def gBk(b, ng, k):
                    return (b * NGB + ng) * NKI + k

                def gCk(b, n, k):
                    return (b * NO_C + n) * NKH + k

                def lne(b, phase, s):
                    return (b * 2 + phase) * NSB + s

                @blk.sync
                def _(sy):
                    semG0.inc(sy.dma_start(out=s_identf[:], in_=identf_in[:]), 16)
                    sy.wait_ge(semSM.h, semSM.n)
                    sy.wait_ge(semWD.h, WD_FULL)
                    for b in range(NBLK):
                        for ng in range(NGB):
                            for k in range(NKI):
                                gi = gBk(b, ng, k)
                                if gi >= NS_M:
                                    sy.wait_ge(semBk.h, gi - NS_M + 1)
                                mslot = gi % NS_M
                                if BC_NO_DMATP:
                                    semBPm[mslot].inc(sy.dma_start(
                                        out=b_rM[:, mslot * 512:(mslot + 1) * 512],
                                        in_=spM[0:128, 0:512]), 16)
                                else:
                                    semBPm[mslot].inc(sy.dma_start_transpose(
                                        out=b_rM[:, mslot * 512:(mslot + 1) * 512],
                                        in_=spM[b * RB:(b + 1) * RB,
                                                k * 128:(k + 1) * 128]), 16)
                                if gi >= NS_D:
                                    sy.wait_ge(semBk.h, gi - NS_D + 1)
                                dslot = gi % NS_D
                                semBPd[dslot].inc(sy.dma_start(
                                    out=b_rD[:, dslot * PAIR * 512:
                                             (dslot + 1) * PAIR * 512],
                                    in_=wdT_bf[k * 128:(k + 1) * 128,
                                               ng * PAIR * 512:(ng + 1) * PAIR * 512]), 16)

                @blk.scalar
                def _(sc):
                    sc.wait_ge(semAG.h, semAG.n)
                    LVL = BC_LEVEL
                    for b in range(NBLK):
                        for ng in range(NGB):
                            gng = b * NGB + ng
                            if gng >= 1:
                                sc.wait_ge(semBE.h, gng * NSB * PAIR)
                            semXB2[gng % 2].inc(sc.dma_start(
                                out=b_xp[:].rearrange("p (s n) -> p s n", s=NSB),
                                in_=x_in[b * RB:(b + 1) * RB,
                                         ng * PAIR * 512:(ng + 1) * PAIR * 512]
                                    .rearrange("(s p) n -> p s n", p=128)), 16)
                        if LVL < 2:
                            continue
                        for s in range(NSB):
                            sc.wait_ge(semBk.h, (b * NGB + NGB) * NKI)
                            sc.wait_ge(semBE.h, (b * NGB + NGB) * NSB * PAIR)
                            if b > 0 and LVL >= 5:
                                sc.wait_ge(semOut.h, 16 * b * NSB)
                            semS2.inc(sc.activation(
                                b_x1[:, s * H:(s + 1) * H],
                                b_z[:, s * H:(s + 1) * H], AF.Square,
                                accum_out=c_S2[:]))
                            sc.wait_ge(semS2.h, semS2.n)
                            e = lne(b, 0, s) if LVL >= 5 else b * NSB + s
                            sc.wait_ge(semST.h, e + 1)
                            semRS.inc(sc.activation(c_L[:], c_VE[:], AF.Ln))
                            sc.wait_ge(semRS.h, semRS.n)
                            semRS.inc(sc.activation(c_R0[:], c_L[:], AF.Exp, scale=-0.5))
                        for s in range(NSB):
                            if LVL < 3:
                                break
                            for k in range(NKH):
                                gi = (b * NSB + s) * NKH + k
                                sc.wait_ge(semTr.h, gi + 1)
                                semTc.inc(sc.copy(
                                    b_x1T[:, s * H + k * 128:s * H + k * 128 + 128],
                                    pbank[:, 3072 + (gi % 2) * 512:
                                          3072 + (gi % 2) * 512 + 128]))
                        for n in range(NO_C if LVL >= 4 else 0):
                            for k in range(NKH):
                                gi = gCk(b, n, k)
                                if gi >= NS_L:
                                    sc.wait_ge(semCk.h, gi - NS_L + 1)
                                lslot = gi % NS_L
                                if BC_NO_DMATP:
                                    semCPl[lslot].inc(sc.dma_start(
                                        out=b_rL[:, lslot * 512:(lslot + 1) * 512],
                                        in_=wl_nat[0:128, 0:512]), 16)
                                else:
                                    semCPl[lslot].inc(sc.dma_start_transpose(
                                        out=b_rL[:, lslot * 512:(lslot + 1) * 512],
                                        in_=wl_nat[n * 512:(n + 1) * 512,
                                                   k * 128:(k + 1) * 128]), 16)
                                if gi >= NS_H:
                                    sc.wait_ge(semCk.h, gi - NS_H + 1)
                                hslot = gi % NS_H
                                if BC_NO_DMATP:
                                    semCPh[hslot].inc(sc.dma_start(
                                        out=b_rH[:, hslot * 512:(hslot + 1) * 512],
                                        in_=wh_nat[0:128, 0:512]), 16)
                                else:
                                    semCPh[hslot].inc(sc.dma_start_transpose(
                                        out=b_rH[:, hslot * 512:(hslot + 1) * 512],
                                        in_=wh_nat[n * 512:(n + 1) * 512,
                                                   k * 128:(k + 1) * 128]), 16)
                        for s in range(NSB if LVL >= 5 else 0):
                            sc.wait_ge(semCE.h, (b * NO_C + NO_C) * NSB)
                            semS2.inc(sc.activation(
                                b_x1[:, s * H:(s + 1) * H],
                                b_z[:, s * H:(s + 1) * H], AF.Square,
                                accum_out=c_S2[:]))
                            sc.wait_ge(semS2.h, semS2.n)
                            e = lne(b, 1, s)
                            sc.wait_ge(semST.h, e + 1)
                            semRS.inc(sc.activation(c_L[:], c_VE[:], AF.Ln))
                            sc.wait_ge(semRS.h, semRS.n)
                            semRS.inc(sc.activation(c_R0[:], c_L[:], AF.Exp, scale=-0.5))
                        for s in range(NSB if LVL >= 5 else 0):
                            sc.wait_ge(semLNo.h, b * NSB + s + 1)
                            r0 = b * RB + s * 128
                            semOut.inc(sc.dma_start(out=out_t[r0:r0 + 128, :],
                                                    in_=b_x1[:, s * H:(s + 1) * H]), 16)
                    if LVL >= 5:
                        sc.wait_ge(semOut.h, 16 * NSUB)
                    else:
                        sc.wait_ge(semBk.h, semBk.n if semBk.n else 0)

                @blk.tensor
                def _(te):
                    te.wait_ge(semG0.h, 16)
                    te.wait_ge(semCst.h, 16)
                    LVL = BC_LEVEL
                    for b in range(NBLK):
                        for ng in range(NGB):
                            for k in range(NKI):
                                gi = gBk(b, ng, k)
                                mslot = gi % NS_M
                                dslot = gi % NS_D
                                te.wait_ge(semBPm[mslot].h, 16 * (gi // NS_M + 1))
                                te.wait_ge(semBPd[dslot].h, 16 * (gi // NS_D + 1))
                                for nl in range(PAIR):
                                    for s in range(NSB):
                                        j = nl * NSB + s
                                        if k == 0:
                                            if b * NGB + ng > 0:
                                                prev = b * NGB + ng - 1
                                                te.wait_ge(semBE.h, (prev + 1) * NSB * PAIR)
                                            if ng == 0 and b > 0 and LVL >= 4:
                                                te.wait_ge(semCE.h, b * NO_C * NSB)
                                            if ng == 0 and b > 0 and LVL == 3:
                                                te.wait_ge(semTc.h, b * NSB * NKH)
                                        inst = te.matmul(
                                            pbank[:, j * 512:(j + 1) * 512],
                                            b_rM[:, mslot * 512 + s * 128:
                                                 mslot * 512 + s * 128 + 128],
                                            b_rD[:, dslot * PAIR * 512 + nl * 512:
                                                 dslot * PAIR * 512 + (nl + 1) * 512],
                                            start=(k == 0), stop=(k == NKI - 1))
                                semBk.inc(inst)
                        for s in range(NSB if LVL >= 3 else 0):
                            te.wait_ge(semLNx.h, b * NSB + s + 1)
                            for k in range(NKH):
                                gi = (b * NSB + s) * NKH + k
                                if gi >= 2:
                                    te.wait_ge(semTc.h, gi - 1)
                                if BC_NO_TP:
                                    semTr.inc(te.matmul(
                                        pbank[:, 3072 + (gi % 2) * 512:
                                              3072 + (gi % 2) * 512 + 128],
                                        s_identf[:], s_identf[:],
                                        start=True, stop=True))
                                else:
                                    semTr.inc(te.transpose(
                                        pbank[:, 3072 + (gi % 2) * 512:
                                              3072 + (gi % 2) * 512 + 128],
                                        b_x1[:, s * H + k * 128:s * H + (k + 1) * 128],
                                        s_identf[:]))
                        if LVL < 4:
                            continue
                        te.wait_ge(semTc.h, (b + 1) * NSB * NKH)
                        for n in range(NO_C):
                            for k in range(NKH):
                                gi = gCk(b, n, k)
                                lslot = gi % NS_L
                                hslot = gi % NS_H
                                te.wait_ge(semCPl[lslot].h, 16 * (gi // NS_L + 1))
                                te.wait_ge(semCPh[hslot].h, 16 * (gi // NS_H + 1))
                                for s in range(NSB):
                                    for mx in range(2):
                                        j = s * 2 + mx
                                        if k == 0:
                                            if n > 0 or b > 0:
                                                pe = (b * NO_C + n - 1) * NSB
                                                te.wait_ge(semCE.h, pe + NSB)
                                        ring = b_rL if mx == 0 else b_rH
                                        rslot = lslot if mx == 0 else hslot
                                        inst = te.matmul(
                                            pbank[:, j * 512:(j + 1) * 512],
                                            b_x1T[:, s * H + k * 128:
                                                  s * H + k * 128 + 128],
                                            ring[:, rslot * 512:(rslot + 1) * 512],
                                            start=(k == 0),
                                            stop=(BC_NO_BIAS and k == NKH - 1))
                                if k < NKH - 1 or BC_NO_BIAS:
                                    semCk.inc(inst)
                            if not BC_NO_BIAS:
                                for s in range(NSB):
                                    for mx in range(2):
                                        j = s * 2 + mx
                                        inst = te.matmul(
                                            pbank[:, j * 512:(j + 1) * 512],
                                            s_ones[:],
                                            s_biasb[:, n * 512:(n + 1) * 512],
                                            start=False, stop=True)
                                semCk.inc(inst)
                    assert LVL < 4 or semCk.n == NBLK * NO_C * NKH

                @blk.vector
                def _(vec):
                    def inc(ctr, inst):
                        ctr.inc(inst)
                        vec.wait_ge(ctr.h, ctr.n)
                    semVch = semMsk
                    LVL = BC_LEVEL
                    for b in range(NBLK):
                        for ng in range(NGB):
                            gng = b * NGB + ng
                            vec.wait_ge(semBk.h, (gng + 1) * NKI)
                            vec.wait_ge(semXB2[gng % 2].h, 16 * (gng // 2 + 1))
                            xbase = 0
                            for nl in range(PAIR):
                                for s in range(NSB):
                                    j = nl * NSB + s
                                    if b > 0 and ng == 0 and LVL >= 5:
                                        vec.wait_ge(semLNo.h, b * NSB)
                                    inc(semBE, vec.tensor_tensor(
                                        b_z[:, s * H + (ng * PAIR + nl) * 512:
                                            s * H + (ng * PAIR + nl) * 512 + 512],
                                        pbank[:, j * 512:(j + 1) * 512],
                                        b_xp[:, xbase + (s * PAIR + nl) * 512:
                                             xbase + (s * PAIR + nl) * 512 + 512],
                                        op=ALU.add))
                        for s in range(NSB if LVL >= 2 else 0):
                            e = lne(b, 0, s) if LVL >= 5 else b * NSB + s
                            z = b_z[:, s * H:(s + 1) * H]
                            vec.wait_ge(semBE.h, (b * NGB + NGB) * NSB * PAIR)
                            inc(semMsk, vec.tensor_reduce(c_S1[:], z, axis=AX.X,
                                                          op=ALU.add))
                            vec.wait_ge(semS2.h, e + 1)
                            inc(semMsk, vec.tensor_scalar(c_M[:], c_S1[:], 1.0 / H, None,
                                                          op0=ALU.mult))
                            inc(semMsk, vec.tensor_scalar(c_T[:], c_S2[:], 1.0 / H, None,
                                                          op0=ALU.mult))
                            inc(semMsk, vec.tensor_tensor(c_V[:], c_M[:], c_M[:],
                                                          op=ALU.mult))
                            inc(semMsk, vec.tensor_tensor(c_V[:], c_T[:], c_V[:],
                                                          op=ALU.subtract))
                            semST.inc(vec.tensor_scalar(c_VE[:], c_V[:], EPS, None,
                                                        op0=ALU.add))
                            assert LVL < 5 or semST.n == e + 1
                            vec.wait_ge(semST.h, semST.n)
                            vec.wait_ge(semNW.h, 4 * (e + 1))
                            semLNx.inc(vec.tensor_scalar(
                                b_x1[:, s * H:(s + 1) * H], z, c_M[:], c_R1[:],
                                op0=ALU.subtract, op1=ALU.mult))
                            vec.wait_ge(semLNx.h, semLNx.n)
                            if LVL >= 3:
                                inc(semMsk, vec.tensor_tensor(c_SSQ[:, s:s + 1],
                                                              c_R1[:], c_R1[:],
                                                              op=ALU.mult))
                                inc(semMsk, vec.tensor_tensor(c_SSQ[:, s:s + 1],
                                                              c_SSQ[:, s:s + 1],
                                                              c_V[:], op=ALU.mult))
                                inc(semMsk, vec.tensor_scalar(c_mask[:, s:s + 1],
                                                              c_SSQ[:, s:s + 1], 1.0,
                                                              None, op0=ALU.is_gt))
                                inc(semMsk, vec.tensor_scalar(c_inv[:, s:s + 1],
                                                              c_mask[:, s:s + 1],
                                                              -1.0, 1.0,
                                                              op0=ALU.mult,
                                                              op1=ALU.add))
                        for n in range(NO_C if LVL >= 4 else 0):
                            vec.wait_ge(semCk.h, (b * NO_C + n + 1) * NKH)
                            for s in range(NSB):
                                inc(semMsk, vec.tensor_scalar(
                                    b_t1[:], pbank[:, (s * 2) * 512:(s * 2 + 1) * 512],
                                    c_inv[:, s:s + 1], None, op0=ALU.mult))
                                inc(semMsk, vec.scalar_tensor_tensor(
                                    b_t1[:], pbank[:, (s * 2 + 1) * 512:(s * 2 + 2) * 512],
                                    c_mask[:, s:s + 1], b_t1[:],
                                    op0=ALU.mult, op1=ALU.add))
                                semCE.inc(vec.tensor_tensor(
                                    b_z[:, s * H + n * 512:s * H + n * 512 + 512],
                                    b_t1[:],
                                    b_x1[:, s * H + n * 512:s * H + n * 512 + 512],
                                    op=ALU.add))
                                vec.wait_ge(semCE.h, semCE.n)
                        for s in range(NSB if LVL >= 5 else 0):
                            e = lne(b, 1, s)
                            z = b_z[:, s * H:(s + 1) * H]
                            vec.wait_ge(semCE.h, (b * NO_C + NO_C) * NSB)
                            inc(semMsk, vec.tensor_reduce(c_S1[:], z, axis=AX.X,
                                                          op=ALU.add))
                            vec.wait_ge(semS2.h, e + 1)
                            inc(semMsk, vec.tensor_scalar(c_M[:], c_S1[:], 1.0 / H, None,
                                                          op0=ALU.mult))
                            inc(semMsk, vec.tensor_scalar(c_T[:], c_S2[:], 1.0 / H, None,
                                                          op0=ALU.mult))
                            inc(semMsk, vec.tensor_tensor(c_V[:], c_M[:], c_M[:],
                                                          op=ALU.mult))
                            inc(semMsk, vec.tensor_tensor(c_V[:], c_T[:], c_V[:],
                                                          op=ALU.subtract))
                            semST.inc(vec.tensor_scalar(c_VE[:], c_V[:], EPS, None,
                                                        op0=ALU.add))
                            assert LVL < 5 or semST.n == e + 1
                            vec.wait_ge(semST.h, semST.n)
                            vec.wait_ge(semNW.h, 4 * (e + 1))
                            semLNo.inc(vec.tensor_scalar(
                                b_x1[:, s * H:(s + 1) * H], z, c_M[:], c_R1[:],
                                op0=ALU.subtract, op1=ALU.mult))
                            vec.wait_ge(semLNo.h, semLNo.n)

                @blk.gpsimd
                def _(gp):
                    semCst.inc(gp.dma_start(out=s_biasb[:], in_=bias_in[:]), 16)
                    gp.memset(s_ones[:], 1.0)
                    for b in range(NBLK):
                        ph2max = 2 if BC_LEVEL >= 5 else (1 if BC_LEVEL >= 2 else 0)
                        for phase2 in range(ph2max):
                            for s in range(NSB):
                                e = (lne(b, phase2, s) if BC_LEVEL >= 5
                                     else b * NSB + s)
                                gp.wait_ge(semRS.h, 2 * (e + 1))
                                semNW.inc(gp.tensor_tensor(c_R1[:], c_R0[:], c_R0[:],
                                                           op=ALU.mult))
                                gp.wait_ge(semNW.h, semNW.n)
                                semNW.inc(gp.tensor_tensor(c_R1[:], c_R1[:], c_VE[:],
                                                           op=ALU.mult))
                                gp.wait_ge(semNW.h, semNW.n)
                                semNW.inc(gp.tensor_scalar(c_R1[:], c_R1[:], -0.5, 1.5,
                                                           op0=ALU.mult, op1=ALU.add))
                                gp.wait_ge(semNW.h, semNW.n)
                                semNW.inc(gp.tensor_tensor(c_R1[:], c_R0[:], c_R1[:],
                                                           op=ALU.mult))
                                gp.wait_ge(semNW.h, semNW.n)

    return nc


def _host_inputs(inputs, cfg):
    R, H, II, O = cfg["R"], cfg["H"], cfg["II"], cfg["O"]
    ncores = cfg["ncores"]
    OSH = cfg["OSH"]
    x = np.ascontiguousarray(np.asarray(inputs["L0_attn_x"],
                                        dtype=np.float32).reshape(ncores * R, H))
    xT = np.ascontiguousarray(x.T)
    wgT = np.ascontiguousarray(np.asarray(inputs["L0_attn_w_gate"], dtype=np.float32).T)
    wuT = np.ascontiguousarray(np.asarray(inputs["L0_attn_w_up"], dtype=np.float32).T)
    wdT = np.ascontiguousarray(np.asarray(inputs["L0_attn_w_down"], dtype=np.float32).T)
    qw = np.ascontiguousarray(np.asarray(inputs["L0_ffn_qweight"], dtype=np.int32))
    lut4 = np.ascontiguousarray(np.asarray(inputs["L0_ffn_lut4"], dtype=np.float32))
    lut8 = np.ascontiguousarray(np.asarray(inputs["L0_ffn_lut8"], dtype=np.float32))
    bias = np.ascontiguousarray(np.asarray(inputs["L0_ffn_bias"],
                                           dtype=np.float32).reshape(1, O))
    identf = np.eye(128, dtype=np.float32)
    maps1 = []
    for c in range(ncores):
        maps1.append(dict(
            xT=np.ascontiguousarray(xT[:, c * R:(c + 1) * R]),
            wgT=wgT, wuT=wuT, wdT=wdT,
            qw=np.ascontiguousarray(qw[:, c * OSH:(c + 1) * OSH, :]),
            lut4=np.ascontiguousarray(lut4[c * OSH:(c + 1) * OSH, :]),
            lut8=np.ascontiguousarray(lut8[c * OSH:(c + 1) * OSH, :]),
            bias=bias, identf=identf,
        ))
    return x, bias, identf, maps1


def kernel(**inputs):
    cfg = make_cfg()
    ncores, R, H = cfg["ncores"], cfg["R"], cfg["H"]
    x, bias, identf, maps1 = _host_inputs(inputs, cfg)
    nc1 = build(cfg, "AS")
    res1 = run_bass_kernel_spmd(nc1, maps1, core_ids=list(range(ncores)))
    wl_nat = np.ascontiguousarray(np.concatenate(
        [np.asarray(res1.results[c]["wl_sh"]) for c in range(ncores)], axis=0))
    wh_nat = np.ascontiguousarray(np.concatenate(
        [np.asarray(res1.results[c]["wh_sh"]) for c in range(ncores)], axis=0))
    wdT_bf = np.ascontiguousarray(np.asarray(res1.results[0]["wdT_bf"]))
    maps2 = []
    for c in range(ncores):
        maps2.append(dict(
            x=np.ascontiguousarray(x[c * R:(c + 1) * R]),
            spM=np.ascontiguousarray(np.asarray(res1.results[c]["spM"])),
            wdT_bf=wdT_bf, wl_nat=wl_nat, wh_nat=wh_nat,
            bias=bias, identf=identf,
        ))
    nc2 = build(cfg, "BC")
    res2 = run_bass_kernel_spmd(nc2, maps2, core_ids=list(range(ncores)))
    shards = [np.asarray(res2.results[c]["out"], dtype=np.float32)
              for c in range(ncores)]
    return np.concatenate(shards, axis=0).reshape(1, ncores * R, H)


if __name__ == "__main__":
    n1 = build(make_cfg(), "AS")
    print("AS built, instructions:", n1.n_instructions())
    n2 = build(make_cfg(), "BC")
    print("BC built, instructions:", n2.n_instructions())


# version 09 (final)
# speedup vs baseline: 1.0471x; 1.0471x over previous
import os as _os
BC_NO_BIAS = _os.environ.get("BC_NO_BIAS") == "1"
BC_NO_TP = _os.environ.get("BC_NO_TP") == "1"
BC_NO_DMATP = _os.environ.get("BC_NO_DMATP") == "1"
BC_LEVEL = int(_os.environ.get("BC_LEVEL", "5"))
BC_L3SUB = _os.environ.get("BC_L3SUB", "")
import sys
if "/opt/trn_rl_repo" not in sys.path:
    sys.path.insert(0, "/opt/trn_rl_repo")
from contextlib import ExitStack

import numpy as np
import ml_dtypes

import concourse.bass as bass
import concourse.mybir as mybir
from concourse.bass_utils import run_bass_kernel_spmd

F32 = mybir.dt.float32
BF16 = mybir.dt.bfloat16
I32 = mybir.dt.int32
AF = mybir.ActivationFunctionType
ALU = mybir.AluOpType
AX = mybir.AxisListType

NSTEP = 30
EPS = 1e-5
OPS_D_TILE = 54 + 1 + 16 * 2 + 1 + 256 * 2


def make_cfg(ncores=8, R=2048, H=4096, II=11008, O=4096):
    c = {}
    c["ncores"], c["R"], c["H"], c["II"], c["O"] = ncores, R, H, II, O
    c["HALF"] = R // 2
    c["SUBH"] = c["HALF"] // 128
    assert c["SUBH"] % 2 == 0 or c["SUBH"] == 1
    c["GSZ"] = max(1, c["SUBH"] // 2)
    c["NSG"] = c["SUBH"] // c["GSZ"]
    c["NKH"] = H // 128
    ch = []
    i0 = 0
    while i0 < II:
        ch.append((i0, min(512, II - i0)))
        i0 += 512
    c["CHUNKS"] = ch
    c["NCH"] = len(ch)
    c["W32"] = H // 32
    c["OSH"] = O // ncores
    c["NOT"] = c["OSH"] // 128
    c["KTOP"] = II // 2
    c["KGRP"] = 4 if c["NKH"] % 4 == 0 else c["NKH"]
    c["NGRP"] = c["NKH"] // c["KGRP"]
    c["RB"] = min(512, R)
    c["NBLK"] = R // c["RB"]
    c["NSB"] = c["RB"] // 128
    c["NH_B"] = H // 512
    c["PAIR"] = 2 if (c["NH_B"] % 2 == 0 and c["NSB"] * 2 <= 8) else 1
    c["NGB"] = c["NH_B"] // c["PAIR"]
    c["NKI"] = II // 128
    c["NO_C"] = O // 512
    c["NS_A"] = 3
    c["NS_M"] = 3
    c["NS_D"] = 3
    c["NS_L"] = 3
    c["NS_H"] = 3
    c["NSUB"] = R // 128
    return c


class Ctr:
    def __init__(self, handle, name):
        self.h = handle
        self.n = 0
        self.name = name

    def inc(self, inst, k=1):
        inst.then_inc(self.h, k)
        self.n += k
        return self.n


def build(cfg, phase="AS"):
    R, H, II, O = cfg["R"], cfg["H"], cfg["II"], cfg["O"]
    HALF, SUBH, GSZ, NSG = cfg["HALF"], cfg["SUBH"], cfg["GSZ"], cfg["NSG"]
    NKH, CHUNKS, NCH = cfg["NKH"], cfg["CHUNKS"], cfg["NCH"]
    W32, OSH, NOT, KTOP = cfg["W32"], cfg["OSH"], cfg["NOT"], cfg["KTOP"]
    KGRP, NGRP = cfg["KGRP"], cfg["NGRP"]
    RB, NBLK, NSB, PAIR, NGB, NKI, NO_C = (cfg["RB"], cfg["NBLK"], cfg["NSB"],
                                           cfg["PAIR"], cfg["NGB"], cfg["NKI"],
                                           cfg["NO_C"])
    NS_A, NS_M, NS_D, NS_L, NS_H = (cfg["NS_A"], cfg["NS_M"], cfg["NS_D"],
                                    cfg["NS_L"], cfg["NS_H"])
    NSUB = cfg["NSUB"]
    ncores = cfg["ncores"]

    nc = bass.Bass("TRN2", target_bir_lowering=False)

    if phase == "AS":
        x_in = None
        xT_in = nc.dram_tensor("xT", [H, R], F32, kind="ExternalInput").ap()
        wgT_in = nc.dram_tensor("wgT", [H, II], F32, kind="ExternalInput").ap()
        wuT_in = nc.dram_tensor("wuT", [H, II], F32, kind="ExternalInput").ap()
        wdT_in = nc.dram_tensor("wdT", [II, H], F32, kind="ExternalInput").ap()
        qw_in = nc.dram_tensor("qw", [8, OSH, W32], I32, kind="ExternalInput").ap()
        lut4_in = nc.dram_tensor("lut4", [OSH, 16], F32, kind="ExternalInput").ap()
        lut8_in = nc.dram_tensor("lut8", [OSH, 256], F32, kind="ExternalInput").ap()
        out_t = None
        wgT_bf = nc.dram_tensor("wgT_bf", [H, II], BF16, kind="Internal").ap()
        wuT_bf = nc.dram_tensor("wuT_bf", [H, II], BF16, kind="Internal").ap()
        wdT_bf = nc.dram_tensor("wdT_bf", [II, H], BF16, kind="ExternalOutput").ap()
        spP = nc.dram_tensor("spP", [R, II], BF16, kind="Internal").ap()
        spA = nc.dram_tensor("spA", [R, II], F32, kind="Internal").ap()
        spM = nc.dram_tensor("spM", [R, II], BF16, kind="ExternalOutput").ap()
        wl_sh = nc.dram_tensor("wl_sh", [OSH, H], BF16, kind="ExternalOutput").ap()
        wh_sh = nc.dram_tensor("wh_sh", [OSH, H], BF16, kind="ExternalOutput").ap()
        wl_nat = wh_nat = None
    else:
        x_in = nc.dram_tensor("x", [R, H], F32, kind="ExternalInput").ap()
        xT_in = wgT_in = wuT_in = wdT_in = qw_in = lut4_in = lut8_in = None
        spM = nc.dram_tensor("spM", [R, II], BF16, kind="ExternalInput").ap()
        wdT_bf = nc.dram_tensor("wdT_bf", [II, H], BF16, kind="ExternalInput").ap()
        wl_nat = nc.dram_tensor("wl_nat", [O, H], BF16, kind="ExternalInput").ap()
        wh_nat = nc.dram_tensor("wh_nat", [O, H], BF16, kind="ExternalInput").ap()
        out_t = nc.dram_tensor("out", [R, H], F32, kind="ExternalOutput").ap()
        spP = spA = wl_sh = wh_sh = None
    bias_in = nc.dram_tensor("bias", [1, O], F32, kind="ExternalInput").ap()
    identf_in = nc.dram_tensor("identf", [128, 128], F32, kind="ExternalInput").ap()

    with ExitStack() as top:
        def mksem(name):
            return Ctr(top.enter_context(nc.semaphore(name)), name)

        s_identf = top.enter_context(nc.sbuf_tensor("s_identf", [128, 128], F32))
        s_ones = top.enter_context(nc.sbuf_tensor("s_ones", [1, 128], BF16))
        s_biasb = top.enter_context(nc.sbuf_tensor("s_biasb", [1, O], BF16))
        pbank = top.enter_context(nc.psum_tensor("pbank", [128, 4096], F32))

        semG0 = mksem("semG0")
        semCst = mksem("semCst")
        semWc = [mksem(f"semWc{i}") for i in range(4)]
        semWD = mksem("semWD")
        semX = mksem("semX")
        semPanS = [mksem(f"semPanS{i}") for i in range(NS_A)]
        semKgrp = mksem("semKgrp")
        semSilu = mksem("semSilu")
        semUC = mksem("semUC")
        semAbs = mksem("semAbs")
        semP = mksem("semP")
        semSpAp = [mksem(f"semSpAp{i}") for i in range(NSG)]
        semSpPp = [mksem(f"semSpPp{i}") for i in range(NSG)]
        semDL = mksem("semDL")
        semIO = mksem("semIO")
        semDV = mksem("semDV")
        semDWl = mksem("semDWl")
        semDWh = mksem("semDWh")
        semAG = mksem("semAG")
        semSL2 = [mksem(f"semSL2{i}") for i in range(2)]
        semSV2 = [mksem(f"semSV2{i}") for i in range(2)]
        semSP2 = mksem("semSP2")
        semSM = mksem("semSM")
        semBPm = [mksem(f"semBPm{i}") for i in range(NS_M)]
        semBPd = [mksem(f"semBPd{i}") for i in range(NS_D)]
        semXB2 = [mksem(f"semXB2{i}") for i in range(2)]
        semBk = mksem("semBk")
        semBE = mksem("semBE")
        semST = mksem("semST")
        semRS = mksem("semRS")
        semNW = mksem("semNW")
        semLNx = mksem("semLNx")
        semSQ = mksem("semSQ")
        semS2 = mksem("semS2")
        semMsk = mksem("semMsk")
        semTr = mksem("semTr")
        semTc = mksem("semTc")
        semCPl = [mksem(f"semCPl{i}") for i in range(NS_L)]
        semCPh = [mksem(f"semCPh{i}") for i in range(NS_H)]
        semCk = mksem("semCk")
        semCE = mksem("semCE")
        semLNo = mksem("semLNo")
        semOut = mksem("semOut")

        OPS_SUB = 2 + NSTEP * 12 + 1

        if phase == "AS":
            with ExitStack() as ph:
                def sb(name, shape, dt, **kw):
                    return ph.enter_context(nc.sbuf_tensor(name, shape, dt, **kw))
                a_xT = sb("a_xT", [128, NKH * HALF], BF16)
                a_ring = sb("a_ring", [128, NS_A * KGRP * 512], BF16)
                s_sb = sb("s_sb", [128, NSG * GSZ * 512], F32)
                u_sb = sb("u_sb", [128, NSG * GSZ * 512], BF16)
                a_st = sb("a_st", [128, NSG * GSZ * 512], F32)
                P_st = sb("P_st", [128, NSG * GSZ * 512], BF16)
                d_qw = sb("d_qw", [128, 8 * W32], I32)
                d_shp = sb("d_shp", [128, 32], I32)
                d_bt = sb("d_bt", [128, H // 2], I32)
                d_acc = sb("d_acc", [128, H // 2], I32)
                d_idx8 = sb("d_idx8", [128, H], BF16)
                d_idx4 = sb("d_idx4", [128, H], BF16)
                d_l4 = sb("d_l4", [128, 16], F32)
                d_l8 = sb("d_l8", [128, 256], F32)
                d_tmp = sb("d_tmp", [128, H], BF16)
                d_wl = sb("d_wl", [128, H], BF16)
                d_wh = sb("d_wh", [128, H], BF16)

                blk = ph.enter_context(nc.Block("phA"))

                NCAST = 2 * NCH
                CPC = 3
                def castv(c, mi):
                    return 16 * (2 * c + mi + 1)

                def cmi(h, c, mi):
                    return (h * NCH + c) * 2 + mi

                def sgi(h, c, sg):
                    return (h * NCH + c) * 2 + sg

                @blk.gpsimd
                def _(gp):
                    semCst.inc(gp.dma_start(out=s_biasb[:], in_=bias_in[:]), 16)
                    gp.memset(s_ones[:], 1.0)
                    semIO.inc(gp.iota(d_shp[:], pattern=[[1, 32]], base=0,
                                      channel_multiplier=0,
                                      allow_small_or_imprecise_dtypes=True))
                    NXG = max(1, NKH // 4)
                    for g in range(NXG):
                        k0 = g * (NKH // NXG)
                        kn = NKH // NXG
                        semX.inc(gp.dma_start(
                            out=a_xT[:, k0 * HALF:(k0 + kn) * HALF]
                                .rearrange("p (a n) -> p a n", a=kn),
                            in_=xT_in[k0 * 128:(k0 + kn) * 128, 0:HALF]
                                .rearrange("(a p) n -> p a n", p=128)), 16)
                    for c in range(min(CPC, NCH)):
                        i0, ic = CHUNKS[c]
                        semWc[c % 4].inc(gp.dma_start(out=wgT_bf[:, i0:i0 + ic],
                                                      in_=wgT_in[:, i0:i0 + ic]), 16)
                        semWc[c % 4].inc(gp.dma_start(out=wuT_bf[:, i0:i0 + ic],
                                                      in_=wuT_in[:, i0:i0 + ic]), 16)
                    def d_loads(ot):
                        o0 = ot * 128
                        semDL.inc(gp.dma_start(
                            out=d_qw[:].rearrange("p (a b) -> p a b", a=8, b=W32),
                            in_=qw_in[:, o0:o0 + 128, :].transpose([1, 0, 2])), 16)
                        semDL.inc(gp.dma_start(out=d_l4[:], in_=lut4_in[o0:o0 + 128, :]), 16)
                        semDL.inc(gp.dma_start(out=d_l8[:], in_=lut8_in[o0:o0 + 128, :]), 16)
                    d_loads(0)
                    nwd = 0
                    d_load_pts = {min(2 * NCH - 1, ot * (2 * NCH) // NOT): ot
                                  for ot in range(1, NOT)}
                    wd_chunk = max(1, II // 8 // 128 * 128)
                    wd_rows = []
                    r0 = 0
                    while r0 < II:
                        wd_rows.append((r0, min(wd_chunk, II - r0)))
                        r0 += wd_chunk
                    for h in range(2):
                        for c in range(NCH):
                            ch = h * NCH + c
                            if h == 0 and c + CPC < NCH:
                                i0, ic = CHUNKS[c + CPC]
                                cs = (c + CPC) % 4
                                semWc[cs].inc(gp.dma_start(out=wgT_bf[:, i0:i0 + ic],
                                                           in_=wgT_in[:, i0:i0 + ic]), 16)
                                semWc[cs].inc(gp.dma_start(out=wuT_bf[:, i0:i0 + ic],
                                                           in_=wuT_in[:, i0:i0 + ic]), 16)
                            if h == 0 and nwd < len(wd_rows) and c >= 2:
                                rr, rn = wd_rows[nwd]
                                semWD.inc(gp.dma_start(out=wdT_bf[rr:rr + rn, :],
                                                       in_=wdT_in[rr:rr + rn, :]), 16)
                                nwd += 1
                            if ch in d_load_pts:
                                ot = d_load_pts[ch]
                                gp.wait_ge(semDV.h, ot * OPS_D_TILE)
                                d_loads(ot)
                            for sg in range(NSG):
                                g_idx = sgi(h, c, sg)
                                gp.wait_ge(semUC.h, g_idx + 1)
                                gp.wait_ge(semSilu.h, g_idx + 1)
                                if ch >= 1:
                                    gp.wait_ge(semSpPp[sg].h, 16 * ch)
                                semP.inc(gp.tensor_tensor(
                                    P_st[:, sg * GSZ * 512:(sg + 1) * GSZ * 512],
                                    s_sb[:, sg * GSZ * 512:(sg + 1) * GSZ * 512],
                                    u_sb[:, sg * GSZ * 512:(sg + 1) * GSZ * 512],
                                    op=ALU.mult))
                        if h == 0:
                            while nwd < len(wd_rows):
                                rr, rn = wd_rows[nwd]
                                semWD.inc(gp.dma_start(out=wdT_bf[rr:rr + rn, :],
                                                       in_=wdT_in[rr:rr + rn, :]), 16)
                                nwd += 1
                            gp.wait_ge(semKgrp.h, 2 * NCH * NGRP)
                            for g in range(NXG):
                                k0 = g * (NKH // NXG)
                                kn = NKH // NXG
                                semX.inc(gp.dma_start(
                                    out=a_xT[:, k0 * HALF:(k0 + kn) * HALF]
                                        .rearrange("p (a n) -> p a n", a=kn),
                                    in_=xT_in[k0 * 128:(k0 + kn) * 128, HALF:R]
                                        .rearrange("(a p) n -> p a n", p=128)), 16)
                    gp.wait_ge(semDV.h, NOT * OPS_D_TILE)
                    gp.wait_ge(semDWl.h, 16 * NOT)
                    gp.wait_ge(semDWh.h, 16 * NOT)
                    gp.wait_ge(semWD.h, semWD.n)
                    assert semP.n == 2 * NCH * NSG

                @blk.sync
                def _(sy):
                    semG0.inc(sy.dma_start(out=s_identf[:], in_=identf_in[:]), 16)
                    for h in range(2):
                        for c in range(NCH):
                            i0, ic = CHUNKS[c]
                            for mi in range(2):
                                nat = wgT_bf if mi == 0 else wuT_bf
                                sy.wait_ge(semWc[c % 4].h, 32 * (c // 4 + 1))
                                for kg in range(NGRP):
                                    gi = cmi(h, c, mi) * NGRP + kg
                                    if gi >= NS_A:
                                        sy.wait_ge(semKgrp.h, gi - NS_A + 1)
                                    slot = gi % NS_A
                                    k0 = kg * KGRP
                                    semPanS[slot].inc(sy.dma_start(
                                        out=a_ring[:, slot * KGRP * 512:
                                                   slot * KGRP * 512 + KGRP * ic]
                                            .rearrange("p (a n) -> p a n", a=KGRP),
                                        in_=nat[k0 * 128:(k0 + KGRP) * 128, i0:i0 + ic]
                                            .rearrange("(a p) n -> p a n", p=128)), 16)

                @blk.tensor
                def _(te):
                    for h in range(2):
                        te.wait_ge(semX.h, 16 * max(1, NKH // 4) * (h + 1))
                        for c in range(NCH):
                            i0, ic = CHUNKS[c]
                            ch = h * NCH + c
                            for mi in range(2):
                                gidx = cmi(h, c, mi)
                                for kg in range(NGRP):
                                    gi = gidx * NGRP + kg
                                    slot = gi % NS_A
                                    te.wait_ge(semPanS[slot].h, 16 * (gi // NS_A + 1))
                                    for kk in range(KGRP):
                                        k = kg * KGRP + kk
                                        for sg in range(NSG):
                                            if k == 0:
                                                if mi == 1:
                                                    te.wait_ge(semSilu.h, sgi(h, c, sg) + 1)
                                                elif ch > 0:
                                                    te.wait_ge(semUC.h, sgi(h, c - 1, sg) + 1
                                                               if c > 0 else
                                                               sgi(h - 1, NCH - 1, sg) + 1)
                                            for s in range(GSZ):
                                                j = sg * GSZ + s
                                                inst = te.matmul(
                                                    pbank[:, j * 512:j * 512 + ic],
                                                    a_xT[:, k * HALF + j * 128:
                                                         k * HALF + j * 128 + 128],
                                                    a_ring[:, slot * KGRP * 512 + kk * ic:
                                                           slot * KGRP * 512 + (kk + 1) * ic],
                                                    start=(k == 0), stop=(k == NKH - 1))
                                    semKgrp.inc(inst)
                    assert semKgrp.n == 4 * NCH * NGRP

                @blk.scalar
                def _(sc):
                    d_store_pts = {min(2 * NCH - 1, (ot + 1) * (2 * NCH) // NOT): ot
                                   for ot in range(NOT)}
                    for h in range(2):
                        for c in range(NCH):
                            i0, ic = CHUNKS[c]
                            ch = h * NCH + c
                            if ch in d_store_pts:
                                ot = d_store_pts[ch]
                                o0 = ot * 128
                                sc.wait_ge(semDV.h, ot * OPS_D_TILE + 54 + 1 + 32)
                                semDWl.inc(sc.dma_start(out=wl_sh[o0:o0 + 128, :],
                                                        in_=d_wl[:]), 16)
                                sc.wait_ge(semDV.h, (ot + 1) * OPS_D_TILE)
                                semDWh.inc(sc.dma_start(out=wh_sh[o0:o0 + 128, :],
                                                        in_=d_wh[:]), 16)
                            for sg in range(NSG):
                                g_idx = sgi(h, c, sg)
                                sc.wait_ge(semKgrp.h, (cmi(h, c, 0) + 1) * NGRP)
                                if g_idx >= NSG:
                                    sc.wait_ge(semP.h, g_idx - NSG + 1)
                                for s in range(GSZ):
                                    j = sg * GSZ + s
                                    inst = sc.activation(
                                        s_sb[:, j * 512:j * 512 + ic],
                                        pbank[:, j * 512:j * 512 + ic], AF.Silu)
                                semSilu.inc(inst)
                                sc.wait_ge(semSilu.h, semSilu.n)
                                if ch >= 1:
                                    sc.wait_ge(semSpAp[sg].h, 16 * ch)
                                for s in range(GSZ):
                                    j = sg * GSZ + s
                                    inst = sc.activation(
                                        a_st[:, j * 512:j * 512 + ic],
                                        s_sb[:, j * 512:j * 512 + ic], AF.Abs)
                                semAbs.inc(inst)
                                sc.wait_ge(semAbs.h, semAbs.n)
                                r0 = h * HALF + sg * GSZ * 128
                                semSpAp[sg].inc(sc.dma_start(
                                    out=spA[r0:r0 + GSZ * 128, i0:i0 + ic]
                                        .rearrange("(s p) n -> p s n", p=128),
                                    in_=a_st[:, sg * GSZ * 512:(sg + 1) * GSZ * 512]
                                        .rearrange("p (s n) -> p s n", n=512)[:, :, 0:ic]), 16)
                            for sg in range(NSG):
                                g_idx = sgi(h, c, sg)
                                sc.wait_ge(semKgrp.h, (cmi(h, c, 1) + 1) * NGRP)
                                if g_idx >= NSG:
                                    sc.wait_ge(semP.h, g_idx - NSG + 1)
                                for s in range(GSZ):
                                    j = sg * GSZ + s
                                    inst = sc.copy(
                                        u_sb[:, j * 512:j * 512 + ic],
                                        pbank[:, j * 512:j * 512 + ic])
                                semUC.inc(inst)
                                sc.wait_ge(semUC.h, semUC.n)
                            for sg in range(NSG):
                                g_idx = sgi(h, c, sg)
                                sc.wait_ge(semP.h, g_idx + 1)
                                r0 = h * HALF + sg * GSZ * 128
                                semSpPp[sg].inc(sc.dma_start(
                                    out=spP[r0:r0 + GSZ * 128, i0:i0 + ic]
                                        .rearrange("(s p) n -> p s n", p=128),
                                    in_=P_st[:, sg * GSZ * 512:(sg + 1) * GSZ * 512]
                                        .rearrange("p (s n) -> p s n", n=512)[:, :, 0:ic]), 16)
                    for ot in range(NOT):
                        if semDWl.n < 16 * (ot + 1):
                            o0 = ot * 128
                            sc.wait_ge(semDV.h, ot * OPS_D_TILE + 54 + 1 + 32)
                            semDWl.inc(sc.dma_start(out=wl_sh[o0:o0 + 128, :],
                                                    in_=d_wl[:]), 16)
                            sc.wait_ge(semDV.h, (ot + 1) * OPS_D_TILE)
                            semDWh.inc(sc.dma_start(out=wh_sh[o0:o0 + 128, :],
                                                    in_=d_wh[:]), 16)
                    assert semDWl.n == 16 * NOT and semDWh.n == 16 * NOT
                    for sg in range(NSG):
                        assert semSpAp[sg].n == 16 * 2 * NCH
                        assert semSpPp[sg].n == 16 * 2 * NCH

                @blk.vector
                def _(vec):
                    def inc(inst):
                        semDV.inc(inst)
                        vec.wait_ge(semDV.h, semDV.n)
                    vec.wait_ge(semIO.h, 1)
                    W2 = W32 // 2
                    H2 = H // 2
                    for ot in range(NOT):
                        vec.wait_ge(semDL.h, 48 * (ot + 1))
                        shp_b = d_shp[:].unsqueeze(1).broadcast_to([128, W2, 32])
                        for hh in range(2):
                            for p in range(8):
                                qb = d_qw[:, p * W32 + hh * W2:p * W32 + (hh + 1) * W2] \
                                    .unsqueeze(2).broadcast_to([128, W2, 32])
                                inc(vec.tensor_tensor(
                                    d_bt[:].rearrange("p (w b) -> p w b", b=32), qb, shp_b,
                                    op=ALU.logical_shift_right))
                                inc(vec.tensor_scalar(d_bt[:], d_bt[:], 1, p,
                                                      op0=ALU.bitwise_and,
                                                      op1=ALU.logical_shift_left))
                                if p == 0:
                                    inc(vec.tensor_copy(d_acc[:], d_bt[:]))
                                else:
                                    inc(vec.tensor_tensor(d_acc[:], d_acc[:], d_bt[:],
                                                          op=ALU.bitwise_or))
                            inc(vec.tensor_copy(d_idx8[:, hh * H2:(hh + 1) * H2], d_acc[:]))
                            inc(vec.tensor_scalar(d_bt[:], d_acc[:], 15, None,
                                                  op0=ALU.bitwise_and))
                            inc(vec.tensor_copy(d_idx4[:, hh * H2:(hh + 1) * H2], d_bt[:]))
                        if ot > 0:
                            vec.wait_ge(semDWl.h, 16 * ot)
                        inc(vec.memset(d_wl[:], 0.0))
                        for cc in range(16):
                            inc(vec.tensor_scalar(d_tmp[:], d_idx4[:], float(cc),
                                                  d_l4[:, cc:cc + 1],
                                                  op0=ALU.is_equal, op1=ALU.mult))
                            inc(vec.tensor_tensor(d_wl[:], d_wl[:], d_tmp[:], op=ALU.add))
                        if ot > 0:
                            vec.wait_ge(semDWh.h, 16 * ot)
                        inc(vec.memset(d_wh[:], 0.0))
                        for cc in range(256):
                            inc(vec.tensor_scalar(d_tmp[:], d_idx8[:], float(cc),
                                                  d_l8[:, cc:cc + 1],
                                                  op0=ALU.is_equal, op1=ALU.mult))
                            inc(vec.tensor_tensor(d_wh[:], d_wh[:], d_tmp[:], op=ALU.add))
                        assert semDV.n == (ot + 1) * OPS_D_TILE


            A_SPA_FULL = [c.n for c in semSpAp]
            A_SPP_FULL = [c.n for c in semSpPp]

            with ExitStack() as ph:
                def sb(name, shape, dt, **kw):
                    return ph.enter_context(nc.sbuf_tensor(name, shape, dt, **kw))
                NH1 = max(128, (II * 13 // 32) // 128 * 128)
                NH2 = II - NH1
                s_a = [sb(f"s_a{i}", [128, II], F32) for i in range(2)]
                s_P = [sb(f"s_P{i}", [128, II], BF16) for i in range(2)]
                s_m = sb("s_m", [128, II], BF16)
                s_tr1 = sb("s_tr1", [128, NH1], F32)
                s_tr2 = sb("s_tr2", [128, NH2], F32)
                s_lo = [sb(f"s_lo{i}", [128, 1], F32) for i in range(2)]
                s_hi = [sb(f"s_hi{i}", [128, 1], F32) for i in range(2)]
                s_mid = [sb(f"s_mid{i}", [128, 1], F32) for i in range(2)]
                s_c1 = [sb(f"s_c1{i}", [128, 1], F32) for i in range(2)]
                s_c2 = [sb(f"s_c2{i}", [128, 1], F32) for i in range(2)]
                s_sel = [sb(f"s_sel{i}", [128, 1], F32) for i in range(2)]
                s_inv = [sb(f"s_inv{i}", [128, 1], F32) for i in range(2)]
                s_d = [sb(f"s_d{i}", [128, 1], F32) for i in range(2)]
                s_nm = [sb(f"s_nm{i}", [128, 1], F32) for i in range(2)]
                s_cs = [sb(f"s_cs{i}", [128, 1], F32) for i in range(2)]

                blk = ph.enter_context(nc.Block("phS"))

                def sv(t):
                    return semSV2[t % 2]

                def base(t):
                    return (t // 2) * OPS_SUB

                @blk.sync
                def _(sy):
                    for sg in range(NSG):
                        sy.wait_ge(semSpAp[sg].h, A_SPA_FULL[sg])
                        sy.wait_ge(semSpPp[sg].h, A_SPP_FULL[sg])
                    for p in range((NSUB + 1) // 2):
                        for t in (2 * p, 2 * p + 1):
                            if t >= NSUB:
                                continue
                            r0 = t * 128
                            if t > 1:
                                sy.wait_ge(sv(t).h, (t // 2) * OPS_SUB)
                            semSL2[t % 2].inc(sy.dma_start(
                                out=s_a[t % 2][:], in_=spA[r0:r0 + 128, :]), 16)
                            semSL2[t % 2].inc(sy.dma_start(
                                out=s_P[t % 2][:], in_=spP[r0:r0 + 128, :]), 16)
                        for t in (2 * p, 2 * p + 1):
                            if t >= NSUB:
                                continue
                            r0 = t * 128
                            sy.wait_ge(sv(t).h, (t // 2 + 1) * OPS_SUB)
                            semSM.inc(sy.dma_start(out=spM[r0:r0 + 128, :],
                                                   in_=s_m[:]), 16)
                    sy.wait_ge(semSM.h, 16 * NSUB)
                    sy.wait_ge(semDWl.h, 16 * NOT)
                    sy.wait_ge(semDWh.h, 16 * NOT)

                @blk.vector
                def _(vec):
                    def inc(t, inst):
                        c = sv(t)
                        c.inc(inst)
                        vec.wait_ge(c.h, c.n)
                    for p in range((NSUB + 1) // 2):
                        ts_ = [2 * p] + ([2 * p + 1] if 2 * p + 1 < NSUB else [])
                        for t in ts_:
                            vec.wait_ge(semSL2[t % 2].h, 32 * (t // 2 + 1))
                            e = t % 2
                            inc(t, vec.memset(s_lo[e][:], 0.0))
                            inc(t, vec.tensor_reduce(s_hi[e][:], s_a[e][:],
                                                     axis=AX.X, op=ALU.max))
                        for i in range(NSTEP):
                            for t in ts_:
                                e = t % 2
                                inc(t, vec.tensor_tensor(s_mid[e][:], s_lo[e][:],
                                                         s_hi[e][:], op=ALU.add))
                                inc(t, vec.tensor_scalar(s_mid[e][:], s_mid[e][:],
                                                         0.5, None, op0=ALU.mult))
                                inc(t, vec.tensor_scalar(s_nm[e][:], s_mid[e][:],
                                                         -1.0, None, op0=ALU.mult))
                            for t in ts_:
                                e = t % 2
                                sv(t).inc(vec.tensor_scalar(
                                    s_tr1[:], s_a[e][:, 0:NH1], s_mid[e][:], None,
                                    op0=ALU.is_le, op1=ALU.add, accum_out=s_c1[e][:]))
                                vec.wait_ge(sv(t).h, sv(t).n)
                            for t in ts_:
                                e = t % 2
                                assert len(ts_) == 2, "NSUB must be even"
                                k = 2 * (p * NSTEP + i) + (1 if t == ts_[0] else 2)
                                vec.wait_ge(semSP2.h, k)
                                vec.wait_ge(sv(t).h, sv(t).n)
                                inc(t, vec.tensor_scalar(s_cs[e][:], s_c2[e][:], -0.5,
                                                         float(NH2) * 0.5,
                                                         op0=ALU.mult, op1=ALU.add))
                                inc(t, vec.tensor_tensor(s_c1[e][:], s_c1[e][:],
                                                         s_cs[e][:], op=ALU.add))
                                inc(t, vec.tensor_scalar(s_sel[e][:], s_c1[e][:],
                                                         float(KTOP) - 0.9, None,
                                                         op0=ALU.is_ge))
                                inc(t, vec.tensor_scalar(s_inv[e][:], s_sel[e][:],
                                                         -1.0, 1.0, op0=ALU.mult,
                                                         op1=ALU.add))
                                inc(t, vec.tensor_tensor(s_d[e][:], s_mid[e][:],
                                                         s_hi[e][:], op=ALU.subtract))
                                inc(t, vec.scalar_tensor_tensor(s_hi[e][:], s_d[e][:],
                                                                s_sel[e][:], s_hi[e][:],
                                                                op0=ALU.mult,
                                                                op1=ALU.add))
                                inc(t, vec.tensor_tensor(s_d[e][:], s_mid[e][:],
                                                         s_lo[e][:], op=ALU.subtract))
                                inc(t, vec.scalar_tensor_tensor(s_lo[e][:], s_d[e][:],
                                                                s_inv[e][:], s_lo[e][:],
                                                                op0=ALU.mult,
                                                                op1=ALU.add))
                        for t in ts_:
                            e = t % 2
                            if t > 0:
                                vec.wait_ge(semSM.h, 16 * t)
                            inc(t, vec.scalar_tensor_tensor(s_m[:], s_a[e][:],
                                                            s_hi[e][:], s_P[e][:],
                                                            op0=ALU.is_gt,
                                                            op1=ALU.mult))
                            assert sv(t).n == (t // 2 + 1) * OPS_SUB, (t, sv(t).n)

                @blk.scalar
                def _(sc):
                    for p in range((NSUB + 1) // 2):
                        ts_ = [2 * p] + ([2 * p + 1] if 2 * p + 1 < NSUB else [])
                        for t in ts_:
                            sc.wait_ge(semSL2[t % 2].h, 32 * (t // 2 + 1))
                        for i in range(NSTEP):
                            for t in ts_:
                                e = t % 2
                                sc.wait_ge(sv(t).h, base(t) + 2 + 12 * i + 3)
                                semSP2.inc(sc.activation(
                                    s_tr2[:], s_a[e][:, NH1:II], AF.Sign,
                                    bias=s_nm[e][:], accum_out=s_c2[e][:]))
                                sc.wait_ge(semSP2.h, semSP2.n)
            S_SM_FULL = semSM.n

        else:
            with ExitStack() as ph:
                def sb(name, shape, dt, **kw):
                    return ph.enter_context(nc.sbuf_tensor(name, shape, dt, **kw))
                b_z = sb("b_z", [128, NSB * H], F32)
                b_x1 = sb("b_x1", [128, NSB * H], F32)
                b_x1T = sb("b_x1T", [128, NSB * H], BF16)
                b_xp = sb("b_xp", [128, NSB * PAIR * 512], F32)
                b_rM = sb("b_rM", [128, NS_M * 512], BF16, is_transpose_dest=True)
                b_rD = sb("b_rD", [128, NS_D * PAIR * 512], BF16)
                b_rL = sb("b_rL", [128, NS_L * 512], BF16, is_transpose_dest=True)
                b_rH = sb("b_rH", [128, NS_H * 512], BF16, is_transpose_dest=True)
                b_t1 = sb("b_t1", [128, 512], F32)
                c_S1 = sb("c_S1", [128, 1], F32)
                c_S2 = sb("c_S2", [128, 1], F32)
                c_M = sb("c_M", [128, 1], F32)
                c_T = sb("c_T", [128, 1], F32)
                c_V = sb("c_V", [128, 1], F32)
                c_VE = sb("c_VE", [128, 1], F32)
                c_L = sb("c_L", [128, 1], F32)
                c_R0 = sb("c_R0", [128, 1], F32)
                c_R1 = sb("c_R1", [128, 1], F32)
                c_SSQ = sb("c_SSQ", [128, NSB], F32)
                c_mask = sb("c_mask", [128, NSB], F32)
                c_inv = sb("c_inv", [128, NSB], F32)

                blk = ph.enter_context(nc.Block("phBC"))

                WD_FULL = semWD.n

                def gBk(b, ng, k):
                    return (b * NGB + ng) * NKI + k

                def gCk(b, n, k):
                    return (b * NO_C + n) * NKH + k

                def lne(b, phase, s):
                    return (b * 2 + phase) * NSB + s

                @blk.sync
                def _(sy):
                    semG0.inc(sy.dma_start(out=s_identf[:], in_=identf_in[:]), 16)
                    sy.wait_ge(semSM.h, semSM.n)
                    sy.wait_ge(semWD.h, WD_FULL)
                    for b in range(NBLK):
                        for ng in range(NGB):
                            for k in range(NKI):
                                gi = gBk(b, ng, k)
                                if gi >= NS_M:
                                    sy.wait_ge(semBk.h, gi - NS_M + 1)
                                mslot = gi % NS_M
                                if BC_NO_DMATP:
                                    semBPm[mslot].inc(sy.dma_start(
                                        out=b_rM[:, mslot * 512:(mslot + 1) * 512],
                                        in_=spM[0:128, 0:512]), 16)
                                else:
                                    semBPm[mslot].inc(sy.dma_start_transpose(
                                        out=b_rM[:, mslot * 512:(mslot + 1) * 512],
                                        in_=spM[b * RB:(b + 1) * RB,
                                                k * 128:(k + 1) * 128]), 16)
                                if gi >= NS_D:
                                    sy.wait_ge(semBk.h, gi - NS_D + 1)
                                dslot = gi % NS_D
                                semBPd[dslot].inc(sy.dma_start(
                                    out=b_rD[:, dslot * PAIR * 512:
                                             (dslot + 1) * PAIR * 512],
                                    in_=wdT_bf[k * 128:(k + 1) * 128,
                                               ng * PAIR * 512:(ng + 1) * PAIR * 512]), 16)

                @blk.scalar
                def _(sc):
                    sc.wait_ge(semAG.h, semAG.n)
                    LVL = BC_LEVEL
                    for b in range(NBLK):
                        for ng in range(NGB):
                            gng = b * NGB + ng
                            if gng >= 1:
                                sc.wait_ge(semBE.h, gng * NSB * PAIR)
                            semXB2[gng % 2].inc(sc.dma_start(
                                out=b_xp[:].rearrange("p (s n) -> p s n", s=NSB),
                                in_=x_in[b * RB:(b + 1) * RB,
                                         ng * PAIR * 512:(ng + 1) * PAIR * 512]
                                    .rearrange("(s p) n -> p s n", p=128)), 16)
                        if LVL < 2:
                            continue
                        for s in range(NSB):
                            sc.wait_ge(semBk.h, (b * NGB + NGB) * NKI)
                            sc.wait_ge(semBE.h, (b * NGB + NGB) * NSB * PAIR)
                            if b > 0 and LVL >= 5:
                                sc.wait_ge(semOut.h, 16 * b * NSB)
                            semS2.inc(sc.activation(
                                b_x1[:, s * H:(s + 1) * H],
                                b_z[:, s * H:(s + 1) * H], AF.Square,
                                accum_out=c_S2[:]))
                            sc.wait_ge(semS2.h, semS2.n)
                            e = lne(b, 0, s) if LVL >= 5 else b * NSB + s
                            sc.wait_ge(semST.h, e + 1)
                            semRS.inc(sc.activation(c_L[:], c_VE[:], AF.Ln))
                            sc.wait_ge(semRS.h, semRS.n)
                            semRS.inc(sc.activation(c_R0[:], c_L[:], AF.Exp, scale=-0.5))
                        for s in range(NSB):
                            if LVL < 3:
                                break
                            for k in range(NKH):
                                gi = (b * NSB + s) * NKH + k
                                sc.wait_ge(semTr.h, gi + 1)
                                semTc.inc(sc.copy(
                                    b_x1T[:, s * H + k * 128:s * H + k * 128 + 128],
                                    pbank[:, 3072 + (gi % 2) * 512:
                                          3072 + (gi % 2) * 512 + 128]))
                        for n in range(NO_C if LVL >= 4 else 0):
                            for k in range(NKH):
                                gi = gCk(b, n, k)
                                if gi >= NS_L:
                                    sc.wait_ge(semCk.h, gi - NS_L + 1)
                                lslot = gi % NS_L
                                if BC_NO_DMATP:
                                    semCPl[lslot].inc(sc.dma_start(
                                        out=b_rL[:, lslot * 512:(lslot + 1) * 512],
                                        in_=wl_nat[0:128, 0:512]), 16)
                                else:
                                    semCPl[lslot].inc(sc.dma_start_transpose(
                                        out=b_rL[:, lslot * 512:(lslot + 1) * 512],
                                        in_=wl_nat[n * 512:(n + 1) * 512,
                                                   k * 128:(k + 1) * 128]), 16)
                                if gi >= NS_H:
                                    sc.wait_ge(semCk.h, gi - NS_H + 1)
                                hslot = gi % NS_H
                                if BC_NO_DMATP:
                                    semCPh[hslot].inc(sc.dma_start(
                                        out=b_rH[:, hslot * 512:(hslot + 1) * 512],
                                        in_=wh_nat[0:128, 0:512]), 16)
                                else:
                                    semCPh[hslot].inc(sc.dma_start_transpose(
                                        out=b_rH[:, hslot * 512:(hslot + 1) * 512],
                                        in_=wh_nat[n * 512:(n + 1) * 512,
                                                   k * 128:(k + 1) * 128]), 16)
                        for s in range(NSB if LVL >= 5 else 0):
                            sc.wait_ge(semCE.h, (b * NO_C + NO_C) * NSB)
                            semS2.inc(sc.activation(
                                b_x1[:, s * H:(s + 1) * H],
                                b_z[:, s * H:(s + 1) * H], AF.Square,
                                accum_out=c_S2[:]))
                            sc.wait_ge(semS2.h, semS2.n)
                            e = lne(b, 1, s)
                            sc.wait_ge(semST.h, e + 1)
                            semRS.inc(sc.activation(c_L[:], c_VE[:], AF.Ln))
                            sc.wait_ge(semRS.h, semRS.n)
                            semRS.inc(sc.activation(c_R0[:], c_L[:], AF.Exp, scale=-0.5))
                        for s in range(NSB if LVL >= 5 else 0):
                            sc.wait_ge(semLNo.h, b * NSB + s + 1)
                            r0 = b * RB + s * 128
                            semOut.inc(sc.dma_start(out=out_t[r0:r0 + 128, :],
                                                    in_=b_x1[:, s * H:(s + 1) * H]), 16)
                    if LVL >= 5:
                        sc.wait_ge(semOut.h, 16 * NSUB)
                    else:
                        sc.wait_ge(semBk.h, semBk.n if semBk.n else 0)

                @blk.tensor
                def _(te):
                    te.wait_ge(semG0.h, 16)
                    te.wait_ge(semCst.h, 16)
                    LVL = BC_LEVEL
                    for b in range(NBLK):
                        for ng in range(NGB):
                            for k in range(NKI):
                                gi = gBk(b, ng, k)
                                mslot = gi % NS_M
                                dslot = gi % NS_D
                                te.wait_ge(semBPm[mslot].h, 16 * (gi // NS_M + 1))
                                te.wait_ge(semBPd[dslot].h, 16 * (gi // NS_D + 1))
                                for nl in range(PAIR):
                                    for s in range(NSB):
                                        j = nl * NSB + s
                                        if k == 0:
                                            if b * NGB + ng > 0:
                                                prev = b * NGB + ng - 1
                                                te.wait_ge(semBE.h, (prev + 1) * NSB * PAIR)
                                            if ng == 0 and b > 0 and LVL >= 4:
                                                te.wait_ge(semCE.h, b * NO_C * NSB)
                                            if ng == 0 and b > 0 and LVL == 3:
                                                te.wait_ge(semTc.h, b * NSB * NKH)
                                        inst = te.matmul(
                                            pbank[:, j * 512:(j + 1) * 512],
                                            b_rM[:, mslot * 512 + s * 128:
                                                 mslot * 512 + s * 128 + 128],
                                            b_rD[:, dslot * PAIR * 512 + nl * 512:
                                                 dslot * PAIR * 512 + (nl + 1) * 512],
                                            start=(k == 0), stop=(k == NKI - 1))
                                semBk.inc(inst)
                        for s in range(NSB if LVL >= 3 else 0):
                            te.wait_ge(semLNx.h, b * NSB + s + 1)
                            for k in range(NKH):
                                gi = (b * NSB + s) * NKH + k
                                if gi >= 2:
                                    te.wait_ge(semTc.h, gi - 1)
                                if BC_NO_TP:
                                    semTr.inc(te.matmul(
                                        pbank[:, 3072 + (gi % 2) * 512:
                                              3072 + (gi % 2) * 512 + 128],
                                        s_identf[:], s_identf[:],
                                        start=True, stop=True))
                                else:
                                    semTr.inc(te.transpose(
                                        pbank[:, 3072 + (gi % 2) * 512:
                                              3072 + (gi % 2) * 512 + 128],
                                        b_x1[:, s * H + k * 128:s * H + (k + 1) * 128],
                                        s_identf[:]))
                        if LVL < 4:
                            continue
                        te.wait_ge(semTc.h, (b + 1) * NSB * NKH)
                        for n in range(NO_C):
                            for k in range(NKH):
                                gi = gCk(b, n, k)
                                lslot = gi % NS_L
                                hslot = gi % NS_H
                                te.wait_ge(semCPl[lslot].h, 16 * (gi // NS_L + 1))
                                te.wait_ge(semCPh[hslot].h, 16 * (gi // NS_H + 1))
                                for s in range(NSB):
                                    for mx in range(2):
                                        j = s * 2 + mx
                                        if k == 0:
                                            if n > 0 or b > 0:
                                                pe = (b * NO_C + n - 1) * NSB
                                                te.wait_ge(semCE.h, pe + NSB)
                                        ring = b_rL if mx == 0 else b_rH
                                        rslot = lslot if mx == 0 else hslot
                                        inst = te.matmul(
                                            pbank[:, j * 512:(j + 1) * 512],
                                            b_x1T[:, s * H + k * 128:
                                                  s * H + k * 128 + 128],
                                            ring[:, rslot * 512:(rslot + 1) * 512],
                                            start=(k == 0),
                                            stop=(BC_NO_BIAS and k == NKH - 1))
                                if k < NKH - 1 or BC_NO_BIAS:
                                    semCk.inc(inst)
                            if not BC_NO_BIAS:
                                for s in range(NSB):
                                    for mx in range(2):
                                        j = s * 2 + mx
                                        inst = te.matmul(
                                            pbank[:, j * 512:(j + 1) * 512],
                                            s_ones[:],
                                            s_biasb[:, n * 512:(n + 1) * 512],
                                            start=False, stop=True)
                                semCk.inc(inst)
                    assert LVL < 4 or semCk.n == NBLK * NO_C * NKH

                @blk.vector
                def _(vec):
                    def inc(ctr, inst):
                        ctr.inc(inst)
                        vec.wait_ge(ctr.h, ctr.n)
                    semVch = semMsk
                    LVL = BC_LEVEL
                    for b in range(NBLK):
                        for ng in range(NGB):
                            gng = b * NGB + ng
                            vec.wait_ge(semBk.h, (gng + 1) * NKI)
                            vec.wait_ge(semXB2[gng % 2].h, 16 * (gng // 2 + 1))
                            xbase = 0
                            for nl in range(PAIR):
                                for s in range(NSB):
                                    j = nl * NSB + s
                                    if b > 0 and ng == 0 and LVL >= 5:
                                        vec.wait_ge(semLNo.h, b * NSB)
                                    inc(semBE, vec.tensor_tensor(
                                        b_z[:, s * H + (ng * PAIR + nl) * 512:
                                            s * H + (ng * PAIR + nl) * 512 + 512],
                                        pbank[:, j * 512:(j + 1) * 512],
                                        b_xp[:, xbase + (s * PAIR + nl) * 512:
                                             xbase + (s * PAIR + nl) * 512 + 512],
                                        op=ALU.add))
                        for s in range(NSB if LVL >= 2 else 0):
                            e = lne(b, 0, s) if LVL >= 5 else b * NSB + s
                            z = b_z[:, s * H:(s + 1) * H]
                            vec.wait_ge(semBE.h, (b * NGB + NGB) * NSB * PAIR)
                            inc(semMsk, vec.tensor_reduce(c_S1[:], z, axis=AX.X,
                                                          op=ALU.add))
                            vec.wait_ge(semS2.h, e + 1)
                            inc(semMsk, vec.tensor_scalar(c_M[:], c_S1[:], 1.0 / H, None,
                                                          op0=ALU.mult))
                            inc(semMsk, vec.tensor_scalar(c_T[:], c_S2[:], 1.0 / H, None,
                                                          op0=ALU.mult))
                            inc(semMsk, vec.tensor_tensor(c_V[:], c_M[:], c_M[:],
                                                          op=ALU.mult))
                            inc(semMsk, vec.tensor_tensor(c_V[:], c_T[:], c_V[:],
                                                          op=ALU.subtract))
                            semST.inc(vec.tensor_scalar(c_VE[:], c_V[:], EPS, None,
                                                        op0=ALU.add))
                            assert LVL < 5 or semST.n == e + 1
                            vec.wait_ge(semST.h, semST.n)
                            vec.wait_ge(semNW.h, 4 * (e + 1))
                            semLNx.inc(vec.tensor_scalar(
                                b_x1[:, s * H:(s + 1) * H], z, c_M[:], c_R1[:],
                                op0=ALU.subtract, op1=ALU.mult))
                            vec.wait_ge(semLNx.h, semLNx.n)
                            if LVL >= 3:
                                inc(semMsk, vec.tensor_tensor(c_SSQ[:, s:s + 1],
                                                              c_R1[:], c_R1[:],
                                                              op=ALU.mult))
                                inc(semMsk, vec.tensor_tensor(c_SSQ[:, s:s + 1],
                                                              c_SSQ[:, s:s + 1],
                                                              c_V[:], op=ALU.mult))
                                inc(semMsk, vec.tensor_scalar(c_mask[:, s:s + 1],
                                                              c_SSQ[:, s:s + 1], 1.0,
                                                              None, op0=ALU.is_gt))
                                inc(semMsk, vec.tensor_scalar(c_inv[:, s:s + 1],
                                                              c_mask[:, s:s + 1],
                                                              -1.0, 1.0,
                                                              op0=ALU.mult,
                                                              op1=ALU.add))
                        for n in range(NO_C if LVL >= 4 else 0):
                            vec.wait_ge(semCk.h, (b * NO_C + n + 1) * NKH)
                            for s in range(NSB):
                                inc(semMsk, vec.tensor_scalar(
                                    b_t1[:], pbank[:, (s * 2) * 512:(s * 2 + 1) * 512],
                                    c_inv[:, s:s + 1], None, op0=ALU.mult))
                                inc(semMsk, vec.scalar_tensor_tensor(
                                    b_t1[:], pbank[:, (s * 2 + 1) * 512:(s * 2 + 2) * 512],
                                    c_mask[:, s:s + 1], b_t1[:],
                                    op0=ALU.mult, op1=ALU.add))
                                semCE.inc(vec.tensor_tensor(
                                    b_z[:, s * H + n * 512:s * H + n * 512 + 512],
                                    b_t1[:],
                                    b_x1[:, s * H + n * 512:s * H + n * 512 + 512],
                                    op=ALU.add))
                                vec.wait_ge(semCE.h, semCE.n)
                        for s in range(NSB if LVL >= 5 else 0):
                            e = lne(b, 1, s)
                            z = b_z[:, s * H:(s + 1) * H]
                            vec.wait_ge(semCE.h, (b * NO_C + NO_C) * NSB)
                            inc(semMsk, vec.tensor_reduce(c_S1[:], z, axis=AX.X,
                                                          op=ALU.add))
                            vec.wait_ge(semS2.h, e + 1)
                            inc(semMsk, vec.tensor_scalar(c_M[:], c_S1[:], 1.0 / H, None,
                                                          op0=ALU.mult))
                            inc(semMsk, vec.tensor_scalar(c_T[:], c_S2[:], 1.0 / H, None,
                                                          op0=ALU.mult))
                            inc(semMsk, vec.tensor_tensor(c_V[:], c_M[:], c_M[:],
                                                          op=ALU.mult))
                            inc(semMsk, vec.tensor_tensor(c_V[:], c_T[:], c_V[:],
                                                          op=ALU.subtract))
                            semST.inc(vec.tensor_scalar(c_VE[:], c_V[:], EPS, None,
                                                        op0=ALU.add))
                            assert LVL < 5 or semST.n == e + 1
                            vec.wait_ge(semST.h, semST.n)
                            vec.wait_ge(semNW.h, 4 * (e + 1))
                            semLNo.inc(vec.tensor_scalar(
                                b_x1[:, s * H:(s + 1) * H], z, c_M[:], c_R1[:],
                                op0=ALU.subtract, op1=ALU.mult))
                            vec.wait_ge(semLNo.h, semLNo.n)

                @blk.gpsimd
                def _(gp):
                    semCst.inc(gp.dma_start(out=s_biasb[:], in_=bias_in[:]), 16)
                    gp.memset(s_ones[:], 1.0)
                    for b in range(NBLK):
                        ph2max = 2 if BC_LEVEL >= 5 else (1 if BC_LEVEL >= 2 else 0)
                        for phase2 in range(ph2max):
                            for s in range(NSB):
                                e = (lne(b, phase2, s) if BC_LEVEL >= 5
                                     else b * NSB + s)
                                gp.wait_ge(semRS.h, 2 * (e + 1))
                                semNW.inc(gp.tensor_tensor(c_R1[:], c_R0[:], c_R0[:],
                                                           op=ALU.mult))
                                gp.wait_ge(semNW.h, semNW.n)
                                semNW.inc(gp.tensor_tensor(c_R1[:], c_R1[:], c_VE[:],
                                                           op=ALU.mult))
                                gp.wait_ge(semNW.h, semNW.n)
                                semNW.inc(gp.tensor_scalar(c_R1[:], c_R1[:], -0.5, 1.5,
                                                           op0=ALU.mult, op1=ALU.add))
                                gp.wait_ge(semNW.h, semNW.n)
                                semNW.inc(gp.tensor_tensor(c_R1[:], c_R0[:], c_R1[:],
                                                           op=ALU.mult))
                                gp.wait_ge(semNW.h, semNW.n)

    return nc


def _host_inputs(inputs, cfg):
    R, H, II, O = cfg["R"], cfg["H"], cfg["II"], cfg["O"]
    ncores = cfg["ncores"]
    OSH = cfg["OSH"]
    x = np.ascontiguousarray(np.asarray(inputs["L0_attn_x"],
                                        dtype=np.float32).reshape(ncores * R, H))
    xT = np.ascontiguousarray(x.T)
    wgT = np.ascontiguousarray(np.asarray(inputs["L0_attn_w_gate"], dtype=np.float32).T)
    wuT = np.ascontiguousarray(np.asarray(inputs["L0_attn_w_up"], dtype=np.float32).T)
    wdT = np.ascontiguousarray(np.asarray(inputs["L0_attn_w_down"], dtype=np.float32).T)
    qw = np.ascontiguousarray(np.asarray(inputs["L0_ffn_qweight"], dtype=np.int32))
    lut4 = np.ascontiguousarray(np.asarray(inputs["L0_ffn_lut4"], dtype=np.float32))
    lut8 = np.ascontiguousarray(np.asarray(inputs["L0_ffn_lut8"], dtype=np.float32))
    bias = np.ascontiguousarray(np.asarray(inputs["L0_ffn_bias"],
                                           dtype=np.float32).reshape(1, O))
    identf = np.eye(128, dtype=np.float32)
    maps1 = []
    for c in range(ncores):
        maps1.append(dict(
            xT=np.ascontiguousarray(xT[:, c * R:(c + 1) * R]),
            wgT=wgT, wuT=wuT, wdT=wdT,
            qw=np.ascontiguousarray(qw[:, c * OSH:(c + 1) * OSH, :]),
            lut4=np.ascontiguousarray(lut4[c * OSH:(c + 1) * OSH, :]),
            lut8=np.ascontiguousarray(lut8[c * OSH:(c + 1) * OSH, :]),
            bias=bias, identf=identf,
        ))
    return x, bias, identf, maps1


def kernel(**inputs):
    cfg = make_cfg()
    ncores, R, H = cfg["ncores"], cfg["R"], cfg["H"]
    x, bias, identf, maps1 = _host_inputs(inputs, cfg)
    nc1 = build(cfg, "AS")
    res1 = run_bass_kernel_spmd(nc1, maps1, core_ids=list(range(ncores)))
    wl_nat = np.ascontiguousarray(np.concatenate(
        [np.asarray(res1.results[c]["wl_sh"]) for c in range(ncores)], axis=0))
    wh_nat = np.ascontiguousarray(np.concatenate(
        [np.asarray(res1.results[c]["wh_sh"]) for c in range(ncores)], axis=0))
    wdT_bf = np.ascontiguousarray(np.asarray(res1.results[0]["wdT_bf"]))
    maps2 = []
    for c in range(ncores):
        maps2.append(dict(
            x=np.ascontiguousarray(x[c * R:(c + 1) * R]),
            spM=np.ascontiguousarray(np.asarray(res1.results[c]["spM"])),
            wdT_bf=wdT_bf, wl_nat=wl_nat, wh_nat=wh_nat,
            bias=bias, identf=identf,
        ))
    nc2 = build(cfg, "BC")
    res2 = run_bass_kernel_spmd(nc2, maps2, core_ids=list(range(ncores)))
    shards = [np.asarray(res2.results[c]["out"], dtype=np.float32)
              for c in range(ncores)]
    return np.concatenate(shards, axis=0).reshape(1, ncores * R, H)


if __name__ == "__main__":
    n1 = build(make_cfg(), "AS")
    print("AS built, instructions:", n1.n_instructions())
    n2 = build(make_cfg(), "BC")
    print("BC built, instructions:", n2.n_instructions())
